# version 28
# baseline (speedup 1.0000x reference)
import numpy as np

import concourse.bass as bass
import concourse.mybir as mybir
from concourse.bass_utils import run_bass_kernel_spmd

F32 = mybir.dt.float32
BF16 = mybir.dt.bfloat16
AF = mybir.ActivationFunctionType
ALU = mybir.AluOpType
AX = mybir.AxisListType

D = 1024
DI = 2048
DS = 16
NCORES = 8
NJ = DI // 128
NK = D // 128


class Ev:
    __slots__ = ("sem", "val")

    def __init__(self, sem, val):
        self.sem = sem
        self.val = val


class TT:
    __slots__ = ("name", "w", "rs")

    def __init__(self, name):
        self.name = name
        self.w = None
        self.rs = []


class Rec:
    def __init__(self, sem):
        self.sem = sem
        self.n = 0
        self.marks = {}
        self.thunks = []

    def wait(self, ev):
        if ev is None:
            return
        key = id(ev.sem)
        if self.marks.get(key, 0) >= ev.val:
            return
        self.marks[key] = ev.val
        s, v = ev.sem, ev.val
        self.thunks.append(lambda e: e.wait_ge(s, v))

    def waits(self, deps):
        for d in deps:
            self.wait(d)

    def op(self, fn, deps=()):
        self.waits(deps)
        self.n += 1
        mysem, myval = self.sem, self.n
        self.thunks.append(lambda e: fn(e).then_inc(mysem, 1))
        return Ev(mysem, myval)


class KB:
    def __init__(self, nc):
        self.nc = nc
        self.dma_counts = {}
        self.pe = Rec(nc.alloc_semaphore("sem_pe"))
        self.dve = Rec(nc.alloc_semaphore("sem_dve"))
        self.act = Rec(nc.alloc_semaphore("sem_act"))
        self.pool = Rec(nc.alloc_semaphore("sem_pool"))
        self.sp = Rec(nc.alloc_semaphore("sem_sp"))

    def dma_sem(self, name):
        s = self.nc.alloc_semaphore(name)
        self.dma_counts[id(s)] = 0
        return s

    @staticmethod
    def _deps(reads, writes):
        deps = []
        for t in reads:
            if t.w is not None:
                deps.append(t.w)
        for t in writes:
            deps.extend(t.rs)
            if t.w is not None:
                deps.append(t.w)
        return deps

    @staticmethod
    def _record(ev, reads, writes):
        for t in reads:
            t.rs.append(ev)
        for t in writes:
            t.w = ev
            t.rs = []
        return ev

    def emit(self, eng, fn, reads=(), writes=()):
        ev = eng.op(fn, self._deps(reads, writes))
        return self._record(ev, reads, writes)

    def dma(self, eng, sem, fn, reads=(), writes=()):
        eng.waits(self._deps(reads, writes))
        self.dma_counts[id(sem)] += 16
        cnt = self.dma_counts[id(sem)]
        eng.thunks.append(lambda e: fn(e).then_inc(sem, 16))
        ev = Ev(sem, cnt)
        return self._record(ev, reads, writes)

    def mm_group(self, items, psum_writes):
        pe = self.pe
        first_deps = []
        for t in psum_writes:
            first_deps.extend(t.rs)
            if t.w is not None:
                first_deps.append(t.w)
        ev = None
        for i, (fn, rts) in enumerate(items):
            deps = list(first_deps) if i == 0 else []
            for t in rts:
                if t.w is not None:
                    deps.append(t.w)
            ev = pe.op(fn, deps)
            for t in rts:
                t.rs.append(ev)
        for t in psum_writes:
            t.w = ev
            t.rs = []
        return ev


class MultiAcc:

    def __init__(self, kb, tiles):
        self.kb = kb
        self.tiles = tiles
        self.started = set()
        self.evs = {}

    def mm(self, fn, reads, gi):
        deps = []
        if gi not in self.started:
            t = self.tiles[gi]
            deps.extend(t.rs)
            if t.w is not None:
                deps.append(t.w)
            self.started.add(gi)
        for t in reads:
            if t.w is not None:
                deps.append(t.w)
        ev = self.kb.pe.op(fn, deps)
        for t in reads:
            t.rs.append(ev)
        self.evs[gi] = ev
        return ev

    def finish(self):
        for gi in self.started:
            t = self.tiles[gi]
            t.w = self.evs[gi]
            t.rs = []



def build_nc(B, T, CH):
    NC_STEPS = T // CH
    NTT = CH // 128
    CS_MAX = CH + 2
    NSTEP = B * NC_STEPS
    nc = bass.Bass("TRN2", target_bir_lowering=False)
    kb = KB(nc)

    inp = nc.dram_tensor("inp", [B, T, D], F32, kind="ExternalInput").ap()
    w_in = nc.dram_tensor("w_in", [D, 2 * DI], F32, kind="ExternalInput").ap()
    w_x = nc.dram_tensor("w_x", [DI, 2 * DS], F32, kind="ExternalInput").ap()
    w_dt = nc.dram_tensor("w_dt", [DI, DI], F32, kind="ExternalInput").ap()
    w_out = nc.dram_tensor("w_out", [DI, D], F32, kind="ExternalInput").ap()
    ffn_w = nc.dram_tensor("ffn_w", [D, D], F32, kind="ExternalInput").ap()
    a_cols = nc.dram_tensor("a_cols", [128, NJ * DS], F32, kind="ExternalInput").ap()
    d_cols = nc.dram_tensor("d_cols", [128, NJ], F32, kind="ExternalInput").ap()
    convw_cols = nc.dram_tensor("convw_cols", [128, NJ * 4], F32, kind="ExternalInput").ap()
    convb_cols = nc.dram_tensor("convb_cols", [128, NJ], F32, kind="ExternalInput").ap()
    bdt_cols = nc.dram_tensor("bdt_cols", [128, NJ], F32, kind="ExternalInput").ap()
    ffnb_cols = nc.dram_tensor("ffnb_cols", [128, NK], F32, kind="ExternalInput").ap()
    alpha_bc = nc.dram_tensor("alpha_bc", [128, 1], F32, kind="ExternalInput").ap()
    ident_in = nc.dram_tensor("ident_in", [128, 128], F32, kind="ExternalInput").ap()
    ones_in = nc.dram_tensor("ones_in", [1, 128], F32, kind="ExternalInput").ap()
    onescol_in = nc.dram_tensor("onescol_in", [128, 1], F32, kind="ExternalInput").ap()
    out = nc.dram_tensor("out", [B, T, D], F32, kind="ExternalOutput").ap()
    win_h = nc.dram_tensor("win_h", [D, 2 * DI], BF16).ap()
    win_l = nc.dram_tensor("win_l", [D, 2 * DI], BF16).ap()
    wdt_h = nc.dram_tensor("wdt_h", [DI, DI], BF16).ap()
    wdt_l = nc.dram_tensor("wdt_l", [DI, DI], BF16).ap()
    wout_h = nc.dram_tensor("wout_h", [DI, D], BF16).ap()
    wout_l = nc.dram_tensor("wout_l", [DI, D], BF16).ap()
    wtT_scr = nc.dram_tensor("wtT_scr", [D, D], BF16).ap()

    S = lambda name, shape, dt=F32: nc.alloc_sbuf_tensor(name, shape, dt)

    x_tm = [S(f"x_tm{i}", [128, D]) for i in range(NTT)]
    ln_sc = S("ln_sc", [128, 16])
    bnst = S("bnst", [128, 12])
    XNW = 2 + CH
    xn_hl = [S(f"xnhl{j}", [128, 2 * XNW], BF16) for j in range(NK)]
    xnh = [t[:, 0:XNW] for t in xn_hl]
    xnl = [t[:, XNW:2 * XNW] for t in xn_hl]
    xi_hl = [S(f"xihl{j}", [128, 2 * CH], BF16) for j in range(NK)]
    xih = [t[:, 0:CH] for t in xi_hl]
    xil = [t[:, CH:2 * CH] for t in xi_hl]
    x_pre = [S(f"xpre{j}", [128, CH + 5]) for j in range(NJ)]
    conv_acc = [S(f"cacc{i}", [128, CS_MAX]) for i in range(2)]
    xf_hl = [S(f"xfhl{j}", [128, 2 * CS_MAX], BF16) for j in range(NJ)]
    xfh = [t[:, 0:CS_MAX] for t in xf_hl]
    xfl = [t[:, CS_MAX:2 * CS_MAX] for t in xf_hl]
    sz_buf = [S(f"sz{j}", [128, 2 + CH]) for j in range(NJ)]
    delta = [S(f"dl{j}", [128, CS_MAX]) for j in range(NJ)]
    wv = [S(f"wvv{j}", [128, CS_MAX]) for j in range(NJ)]
    y_t = [S(f"yy{j}", [128, CS_MAX]) for j in range(NJ)]
    y_bf = [S(f"ybf{j}", [128, 2 * CS_MAX], BF16) for j in range(NJ)]
    ybh = [t[:, 0:CS_MAX] for t in y_bf]
    ybl = [t[:, CS_MAX:2 * CS_MAX] for t in y_bf]
    NRING = 4
    h_rg = S("h_rg", [128, NRING * CS_MAX])
    a_t = [S(f"aa{i}", [128, CS_MAX]) for i in range(NRING)]
    b_t = [S(f"bb{i}", [128, CS_MAX]) for i in range(NRING)]
    cm_t = [S(f"cmm{i}", [128, CS_MAX]) for i in range(2)]
    NBREP = 3
    bcrep = [S(f"bcr{i}", [128, 2 * CS_MAX]) for i in range(NBREP)]
    brow = [S(f"brw{i}", [1, 2 * CS_MAX]) for i in range(2)]

    def hslot(j):
        return h_rg.ap()[:, (j % NRING) * CS_MAX:(j % NRING) * CS_MAX + CS_MAX]
    bc_sb = S("bc_sb", [32, CS_MAX])
    carries = S("carries", [128, NJ * DS])
    tails = S("tails", [128, 3 * NJ])
    s_sb = [[S(f"s_sb{p}_{o}", [128, CS_MAX]) for o in range(NK)] for p in range(2)]
    g_sb = [S(f"g_sb{o}", [128, CS_MAX]) for o in range(NK)]
    ostg = S("ostg0", [128, D])
    wb_h = [S(f"wbh{i}", [128, 512], BF16) for i in range(2)]
    wb_l = [S(f"wbl{i}", [128, 512], BF16) for i in range(2)]
    wx_h = S("wx_h", [128, NJ * 2 * DS], BF16)
    wx_l = S("wx_l", [128, NJ * 2 * DS], BF16)
    acol_sb = S("acol_sb", [128, NJ * DS])
    dcol_sb = S("dcol_sb", [128, NJ])
    cw_sb = S("cw_sb", [128, NJ * 4])
    cb_sb = S("cb_sb", [128, NJ])
    bdt_sb = S("bdt_sb", [128, NJ])
    ffnb_sb = S("ffnb_sb", [128, NK])
    alpha_sb = S("alpha_sb", [128, 1])
    ident_sb = S("ident_sb", [128, 128])
    ones_sb = S("ones_sb", [1, 128])
    onescol_sb = S("onescol_sb", [128, 1])
    thr_sb = S("thr_sb", [128, 2])
    eps_sb = S("eps_sb", [128, 1])
    fscr = x_tm[0]
    ftrn = x_tm[1]

    PS8 = [nc.alloc_psum_tensor(f"ps{i}", [128, 512], F32) for i in range(8)]
    P = PS8

    def pacc(i):
        return PS8[i].ap()

    t_x_tm = [TT(f"x_tm{i}") for i in range(NTT)]
    t_lnsc = TT("ln_sc"); t_bnst = TT("bnst")
    t_xnh = [TT(f"xnh{j}") for j in range(NK)]
    t_xnl = [TT(f"xnl{j}") for j in range(NK)]
    t_xih = [TT(f"xih{j}") for j in range(NK)]
    t_xil = [TT(f"xil{j}") for j in range(NK)]
    t_xpre = [TT(f"xpre{j}") for j in range(NJ)]
    t_cacc = [TT(f"cacc{i}") for i in range(2)]
    t_xfh = [TT(f"xfh{j}") for j in range(NJ)]
    t_xfl = [TT(f"xfl{j}") for j in range(NJ)]
    t_sz = [TT(f"sz{j}") for j in range(NJ)]
    t_dl = [TT(f"dl{j}") for j in range(NJ)]
    t_wv = [TT(f"wv{j}") for j in range(NJ)]
    t_y = [TT(f"y{j}") for j in range(NJ)]
    t_ybh = [TT(f"ybh{j}") for j in range(NJ)]
    t_ybl = [TT(f"ybl{j}") for j in range(NJ)]
    t_h = [TT(f"h{i}") for i in range(NRING)]
    t_a = [TT(f"a{i}") for i in range(NRING)]
    t_b = [TT(f"b{i}") for i in range(NRING)]
    t_cm = [TT(f"cm{i}") for i in range(2)]
    t_bcr = [TT(f"bcr{i}") for i in range(NBREP)]
    t_brow = [TT(f"brow{i}") for i in range(2)]
    t_bc = TT("bc")
    t_car = [[TT(f"car{s}_{g}") for g in range(NJ // 4)] for s in range(DS)]
    t_tails = TT("tails")
    t_s = [[TT(f"s{p}_{o}") for o in range(NK)] for p in range(2)]
    t_g = [TT(f"g{o}") for o in range(NK)]
    t_ostg = TT("ostg0")
    t_wbh = [TT(f"wbh{i}") for i in range(2)]
    t_wbl = [TT(f"wbl{i}") for i in range(2)]
    t_wx = TT("wx"); t_consts = TT("consts")
    t_thr = TT("thr"); t_fscr = t_x_tm[0]; t_ftrn = t_x_tm[1]
    t_P = [TT(f"P{i}") for i in range(8)]
    t_pacc = t_P
    t_out_hbm = TT("out_hbm")
    t_w_win = TT("w_win"); t_w_wdt = TT("w_wdt")
    t_w_wout = TT("w_wout"); t_w_ffn = TT("w_ffn")
    t_chain_h = TT("chain_h"); t_chain_l = TT("chain_l")
    t_fs4 = [TT(f"fs4_{i}") for i in range(4)]

    sem_const = kb.dma_sem("d_const")
    sem_xtm = [kb.dma_sem(f"d_xtm{i}") for i in range(NTT)]
    sem_wh = [kb.dma_sem(f"d_wh{i}") for i in range(2)]
    sem_wl = [kb.dma_sem(f"d_wl{i}") for i in range(2)]
    sem_row = [kb.dma_sem(f"d_row{i}") for i in range(2)]
    sem_oacc = kb.dma_sem("d_oacc0")
    sem_bulk = kb.dma_sem("d_bulk")
    sem_fprep = kb.dma_sem("d_fprep")
    sem_fprep4 = [kb.dma_sem(f"d_fp4_{i}") for i in range(4)]
    sem_fph = kb.dma_sem("d_fph")
    sem_fpl = kb.dma_sem("d_fpl")
    sem_wx = kb.dma_sem("d_wx")

    pe, dve, act, pool, sp = kb.pe, kb.dve, kb.act, kb.pool, kb.sp

    kb.dma(sp, sem_bulk, lambda e: e.dma_start(out=out, in_=inp),
           reads=(), writes=(t_out_hbm,))

    def cdma(fn):
        kb.dma(sp, sem_const, fn, reads=(), writes=(t_consts,))

    cdma(lambda e: e.dma_start(out=acol_sb[:], in_=a_cols))
    cdma(lambda e: e.dma_start(out=dcol_sb[:], in_=d_cols))
    cdma(lambda e: e.dma_start(out=cw_sb[:], in_=convw_cols))
    cdma(lambda e: e.dma_start(out=cb_sb[:], in_=convb_cols))
    cdma(lambda e: e.dma_start(out=bdt_sb[:], in_=bdt_cols))
    cdma(lambda e: e.dma_start(out=ffnb_sb[:], in_=ffnb_cols))
    cdma(lambda e: e.dma_start(out=alpha_sb[:], in_=alpha_bc))
    cdma(lambda e: e.dma_start(out=ident_sb[:], in_=ident_in))
    cdma(lambda e: e.dma_start(out=ones_sb[:], in_=ones_in))
    cdma(lambda e: e.dma_start(out=onescol_sb[:], in_=onescol_in))

    kb.emit(act, lambda e: e.activation(eps_sb[:], onescol_sb[:], AF.Copy,
                                        bias=0.0, scale=1e-5),
            reads=(t_consts,), writes=(t_consts,))

    for k in range(NJ):
        kb.dma(sp, sem_wx,
               lambda e, k=k: e.dma_start(out=fscr[:, 0:2 * DS],
                                          in_=w_x[k * 128:(k + 1) * 128, :]),
               reads=(), writes=(t_fscr,))
        kb.emit(act, lambda e, k=k: e.copy(wx_h[:, k * 2 * DS:(k + 1) * 2 * DS],
                                           fscr[:, 0:2 * DS]),
                reads=(t_fscr,), writes=(t_wx,))
        kb.emit(dve, lambda e, k=k: e.tensor_sub(wx_l[:, k * 2 * DS:(k + 1) * 2 * DS],
                                                 fscr[:, 0:2 * DS],
                                                 wx_h[:, k * 2 * DS:(k + 1) * 2 * DS]),
                reads=(t_fscr, t_wx), writes=(t_wx,))

    pctr = [0]

    def prep_weight(src_ap, dst_h, dst_l, rows, cols, tok):
        NR, NCP = rows // 128, cols // 512
        for r in range(NR):
            for cp in range(NCP):
                first = r == 0 and cp == 0
                last = r == NR - 1 and cp == NCP - 1
                c0 = cp * 512
                p = pctr[0]
                pctr[0] += 1
                fi = p % 4
                f, fc, tf = x_tm[fi // 2], (fi % 2) * 512, t_fs4[fi]
                lwr = (tf, t_x_tm[fi // 2]) if p < 4 else (tf,)
                yi = (2 * p) % 16
                hi_ap = y_bf[yi].ap()[:, 0:512]
                lo_ap = y_bf[yi + 1].ap()[:, 0:512]
                th = (t_ybh[yi], t_ybl[yi])
                tl = (t_ybh[yi + 1], t_ybl[yi + 1])
                kb.dma(sp, sem_fprep4[fi],
                       lambda e, r=r, c0=c0, src_ap=src_ap, f=f, fc=fc: e.dma_start(
                           out=f[:, fc:fc + 512],
                           in_=src_ap[r * 128:(r + 1) * 128, c0:c0 + 512]),
                       reads=(), writes=lwr)
                kb.emit(act, lambda e, f=f, fc=fc, hi_ap=hi_ap: e.copy(
                    hi_ap, f[:, fc:fc + 512]),
                    reads=(tf,), writes=th)
                kb.emit(dve, lambda e, f=f, fc=fc, hi_ap=hi_ap, lo_ap=lo_ap:
                        e.tensor_sub(lo_ap, f[:, fc:fc + 512], hi_ap),
                        reads=(tf,) + th, writes=tl)
                kb.dma(pool, sem_fph,
                       lambda e, r=r, c0=c0, dst_h=dst_h, hi_ap=hi_ap: e.dma_start(
                           out=dst_h[r * 128:(r + 1) * 128, c0:c0 + 512], in_=hi_ap),
                       reads=th + ((t_chain_h,) if first else ()),
                       writes=(tok,) + ((t_chain_h,) if last else ()))
                kb.dma(pool, sem_fpl,
                       lambda e, r=r, c0=c0, dst_l=dst_l, lo_ap=lo_ap: e.dma_start(
                           out=dst_l[r * 128:(r + 1) * 128, c0:c0 + 512], in_=lo_ap),
                       reads=tl + ((t_chain_l,) if first else ()),
                       writes=(tok,) + ((t_chain_l,) if last else ()))

    prep_weight(w_in, win_h, win_l, D, 2 * DI, t_w_win)
    prep_weight(w_dt, wdt_h, wdt_l, DI, DI, t_w_wdt)

    prep_weight(w_out, wout_h, wout_l, DI, D, t_w_wout)

    for k in range(NK):
        kb.dma(sp, sem_fprep,
               lambda e, k=k: e.dma_start(out=fscr[:], in_=ffn_w[k * 128:(k + 1) * 128, :]),
               reads=(), writes=(t_fscr, t_fs4[0], t_fs4[1]))
        kb.emit(dve, lambda e, k=k: e.tensor_reduce(ln_sc.ap()[:, k:k + 1], fscr.ap(),
                                                    AX.X, ALU.add, apply_absolute_value=True),
                reads=(t_fscr,), writes=(t_lnsc,))
    kb.emit(pe, lambda e: e.matmul(P[3].ap()[0:1, 0:8], onescol_sb[:], ln_sc.ap()[:, 0:8],
                                   start=True, stop=True),
            reads=(t_lnsc, t_consts), writes=(t_P[3],))
    kb.emit(dve, lambda e: e.tensor_reduce(ln_sc.ap()[0:1, 8:9], P[3].ap()[0:1, 0:8],
                                           AX.X, ALU.add),
            reads=(t_P[3],), writes=(t_lnsc,))
    kb.emit(pe, lambda e: e.matmul(P[3].ap()[:, 0:1], ones_sb[:], ln_sc.ap()[0:1, 8:9],
                                   start=True, stop=True),
            reads=(t_lnsc, t_consts), writes=(t_P[3],))
    kb.emit(act, lambda e: e.activation(thr_sb[:, 0:1], P[3].ap()[:, 0:1], AF.Copy,
                                        bias=0.0, scale=0.7 / (D * D)),
            reads=(t_P[3],), writes=(t_thr,))
    kb.emit(act, lambda e: e.activation(thr_sb[:, 1:2], thr_sb[:, 0:1], AF.Copy,
                                        bias=0.0, scale=-1.0),
            reads=(t_thr,), writes=(t_thr,))
    for r in range(NK):
        kb.dma(sp, sem_fprep,
               lambda e, r=r: e.dma_start(out=fscr[:], in_=ffn_w[r * 128:(r + 1) * 128, :]),
               reads=(), writes=(t_fscr, t_fs4[0], t_fs4[1]))
        kb.emit(dve, lambda e: e.tensor_scalar(ftrn.ap(), fscr.ap(), thr_sb[:, 0:1], None, ALU.is_gt),
                reads=(t_fscr, t_thr), writes=(t_ftrn, t_fs4[2], t_fs4[3]))
        kb.emit(dve, lambda e: e.tensor_scalar(ostg.ap(), fscr.ap(), thr_sb[:, 1:2], None, ALU.is_lt),
                reads=(t_fscr, t_thr), writes=(t_ostg,))
        kb.emit(dve, lambda e: e.tensor_sub(ftrn.ap(), ftrn.ap(), ostg.ap()),
                reads=(t_ostg, t_ftrn), writes=(t_ftrn,))
        grp = []
        for j in range(NK):
            bank = PS8[6] if j < 4 else PS8[7]
            def fn(e, j=j, bank=bank, first=(j % 4 == 0), last=(j == NK - 1)):
                return e.matmul(bank.ap()[:, (j % 4) * 128:(j % 4 + 1) * 128],
                                ftrn.ap()[:, j * 128:(j + 1) * 128], ident_sb[:],
                                start=first, stop=last, is_transpose=True,
                                skip_group_check=True)
            grp.append((fn, [t_ftrn, t_consts]))
        kb.mm_group(grp, (t_P[6], t_P[7]))
        kb.emit(act, lambda e: e.copy(wb_h[0][:], PS8[6].ap()),
                reads=(t_P[6],), writes=(t_wbh[0],))
        for j in range(4):
            kb.dma(pool, sem_fph,
                   lambda e, j=j, r=r: e.dma_start(
                       out=wtT_scr[j * 128:(j + 1) * 128, r * 128:(r + 1) * 128],
                       in_=wb_h[0][:, j * 128:(j + 1) * 128]),
                   reads=(t_wbh[0],) + ((t_chain_h,) if r == 0 and j == 0 else ()),
                   writes=(t_w_ffn,))
        kb.emit(act, lambda e: e.copy(wb_h[1][:], PS8[7].ap()),
                reads=(t_P[7],), writes=(t_wbh[1],))
        for j in range(4, 8):
            kb.dma(pool, sem_fph,
                   lambda e, j=j, r=r: e.dma_start(
                       out=wtT_scr[j * 128:(j + 1) * 128, r * 128:(r + 1) * 128],
                       in_=wb_h[1][:, (j - 4) * 128:(j - 3) * 128]),
                   reads=(t_wbh[1],), writes=(t_w_ffn,))



    def split_cols(src_fn, hi_ap, lo_ap, reads, whi, wlo):
        kb.emit(dve, lambda e: e.tensor_copy(hi_ap, src_fn()), reads=reads, writes=(whi,))
        kb.emit(dve, lambda e: e.tensor_sub(lo_ap, src_fn(), hi_ap),
                reads=tuple(reads) + (whi,), writes=(wlo,))

    def hl_rhs(hl_handle, W, off, colsS):
        return hl_handle.ap().rearrange("p (s w) -> p s w", w=W)[:, :, off:off + colsS]

    def mm_hl3(ma, gi, whi_ap, wlo_ap, hl_handle, W, off, colsS,
               reads_hi, reads_lo, t_x, first, last, rows=128):
        if 2 * colsS <= 512:
            rhs = hl_rhs(hl_handle, W, off, colsS)
            def f1(e, whi_ap=whi_ap, rhs=rhs, gi=gi, colsS=colsS, first=first, rows=rows):
                return e.matmul(pacc(gi)[0:rows, 0:2 * colsS], whi_ap, rhs,
                                start=first, stop=False, skip_group_check=True)
            ma.mm(f1, list(reads_hi) + list(t_x), gi)
            hi_s = hl_handle.ap()[:, off:off + colsS]
            def f2(e, wlo_ap=wlo_ap, hi_s=hi_s, gi=gi, colsS=colsS, last=last, rows=rows):
                return e.matmul(pacc(gi)[0:rows, 0:colsS], wlo_ap, hi_s,
                                start=False, stop=last, skip_group_check=True)
            ma.mm(f2, list(reads_lo) + list(t_x), gi)
        else:
            hi_s = hl_handle.ap()[:, off:off + colsS]
            lo_s = hl_handle.ap()[:, W + off:W + off + colsS]
            def f1(e, whi_ap=whi_ap, hi_s=hi_s, gi=gi, colsS=colsS, first=first, rows=rows):
                return e.matmul(pacc(gi)[0:rows, 0:colsS], whi_ap, hi_s,
                                start=first, stop=False, skip_group_check=True)
            ma.mm(f1, list(reads_hi) + list(t_x), gi)
            def f2(e, whi_ap=whi_ap, lo_s=lo_s, gi=gi, colsS=colsS, rows=rows):
                return e.matmul(pacc(gi)[0:rows, 0:colsS], whi_ap, lo_s,
                                start=False, stop=False, skip_group_check=True)
            ma.mm(f2, list(reads_hi) + list(t_x), gi)
            def f3(e, wlo_ap=wlo_ap, hi_s=hi_s, gi=gi, colsS=colsS, last=last, rows=rows):
                return e.matmul(pacc(gi)[0:rows, 0:colsS], wlo_ap, hi_s,
                                start=False, stop=last, skip_group_check=True)
            ma.mm(f3, list(reads_lo) + list(t_x), gi)

    def mm_hl1(ma, gi, w_ap, hl_handle, W, off, colsS, reads_w, t_x, first, last):
        if 2 * colsS <= 512:
            rhs = hl_rhs(hl_handle, W, off, colsS)
            def f1(e, w_ap=w_ap, rhs=rhs, gi=gi, colsS=colsS, first=first, last=last):
                return e.matmul(pacc(gi)[:, 0:2 * colsS], w_ap, rhs,
                                start=first, stop=last, skip_group_check=True)
            ma.mm(f1, list(reads_w) + list(t_x), gi)
        else:
            hi_s = hl_handle.ap()[:, off:off + colsS]
            lo_s = hl_handle.ap()[:, W + off:W + off + colsS]
            def f1(e, w_ap=w_ap, hi_s=hi_s, gi=gi, colsS=colsS, first=first):
                return e.matmul(pacc(gi)[:, 0:colsS], w_ap, hi_s,
                                start=first, stop=False, skip_group_check=True)
            ma.mm(f1, list(reads_w) + list(t_x), gi)
            def f2(e, w_ap=w_ap, lo_s=lo_s, gi=gi, colsS=colsS, last=last):
                return e.matmul(pacc(gi)[:, 0:colsS], w_ap, lo_s,
                                start=False, stop=last, skip_group_check=True)
            ma.mm(f2, list(reads_w) + list(t_x), gi)

    def psum2_ap(gi, colsS, rows=128):
        return PS8[gi].ap()[0:rows, 0:2 * colsS].rearrange("p (s w) -> p w s", s=2)

    def evac_sum2(eng, dst_fn, gi, colsS, extra_reads=(), writes=()):
        if 2 * colsS <= 512:
            kb.emit(eng, lambda e: e.tensor_reduce(dst_fn(), psum2_ap(gi, colsS),
                                                   AX.X, ALU.add),
                    reads=(t_pacc[gi],) + tuple(extra_reads), writes=writes)
        else:
            kb.emit(eng, lambda e: e.tensor_copy(dst_fn(), pacc(gi)[:, 0:colsS]),
                    reads=(t_pacc[gi],) + tuple(extra_reads), writes=writes)

    def evac_add2(o, par, gi, colsS):
        halves = [0, colsS] if 2 * colsS <= 512 else [0]
        for a in halves:
            kb.emit(dve, lambda e, o=o, par=par, gi=gi, a=a, colsS=colsS:
                    e.scalar_tensor_tensor(
                        s_sb[par][o][:, 0:colsS], pacc(gi)[:, a:a + colsS],
                        onescol_sb[:, 0:1], s_sb[par][o][:, 0:colsS],
                        ALU.mult, ALU.add),
                    reads=(t_pacc[gi], t_s[par][o], t_consts), writes=(t_s[par][o],))


    def params(n):
        b, c = divmod(n, NC_STEPS)
        s0 = max(0, c * CH - 2)
        e0 = T if c == NC_STEPS - 1 else (c + 1) * CH - 2
        return b, c, s0, e0 - s0, s0 - c * CH + 2, (c == NC_STEPS - 1)

    def pieces_input_ln(n):
        b, c = divmod(n, NC_STEPS)
        ps = []

        def p_dma(b=b, c=c):
            for i in range(NTT):
                kb.dma(sp, sem_xtm[i],
                       lambda e, i=i, b=b, c=c: e.dma_start(
                           out=x_tm[i][:],
                           in_=inp[b, c * CH + i * 128: c * CH + (i + 1) * 128, :]),
                       reads=(), writes=(t_x_tm[i],))
        ps.append(p_dma)

        for i in range(NTT):
            def p_ln(i=i):
                cb8 = 8 * i
                kb.emit(dve, lambda e, i=i: e.bn_stats(bnst.ap()[:, 0:6], x_tm[i][:, 0:512]),
                        reads=(t_x_tm[i],), writes=(t_bnst,))
                kb.emit(dve, lambda e, i=i: e.bn_stats(bnst.ap()[:, 6:12], x_tm[i][:, 512:1024]),
                        reads=(t_x_tm[i],), writes=(t_bnst,))
                kb.emit(dve, lambda e, cb8=cb8: e.bn_aggr(ln_sc.ap()[:, cb8:cb8 + 2], bnst.ap()),
                        reads=(t_bnst,), writes=(t_lnsc,))
                kb.emit(act, lambda e, cb8=cb8: e.activation(ln_sc.ap()[:, cb8 + 2:cb8 + 3],
                                                             ln_sc.ap()[:, cb8:cb8 + 1],
                                                             AF.Copy, bias=0.0, scale=-1.0),
                        reads=(t_lnsc,), writes=(t_lnsc,))
                kb.emit(act, lambda e, cb8=cb8: e.activation(ln_sc.ap()[:, cb8 + 6:cb8 + 7],
                                                             ln_sc.ap()[:, cb8 + 1:cb8 + 2],
                                                             AF.Sqrt, bias=eps_sb[:], scale=1.0),
                        reads=(t_lnsc, t_consts), writes=(t_lnsc,))
                kb.emit(dve, lambda e, cb8=cb8: e.reciprocal(ln_sc.ap()[:, cb8 + 3:cb8 + 4],
                                                             ln_sc.ap()[:, cb8 + 6:cb8 + 7]),
                        reads=(t_lnsc,), writes=(t_lnsc,))
                kb.emit(dve, lambda e, i=i, cb8=cb8: e.tensor_scalar(
                    x_tm[i][:], x_tm[i][:],
                    ln_sc.ap()[:, cb8 + 2:cb8 + 3], ln_sc.ap()[:, cb8 + 3:cb8 + 4],
                    ALU.add, op1=ALU.mult),
                    reads=(t_lnsc, t_x_tm[i]), writes=(t_x_tm[i],))
                kb.emit(dve, lambda e, i=i, cb8=cb8: e.tensor_reduce(
                    ln_sc.ap()[:, cb8 + 4:cb8 + 5], x_tm[i][:], AX.X, ALU.add),
                    reads=(t_x_tm[i],), writes=(t_lnsc,))
                kb.emit(act, lambda e, cb8=cb8: e.activation(
                    ln_sc.ap()[:, cb8 + 5:cb8 + 6], ln_sc.ap()[:, cb8 + 4:cb8 + 5],
                    AF.Copy, bias=0.0, scale=-1.0 / D),
                    reads=(t_lnsc,), writes=(t_lnsc,))
            ps.append(p_ln)
        return ps

    def pieces_transposes(n):
        b, c, s0, colsS, base2, last = params(n)
        ps = []
        for j in range(NK):
            def p(j=j, c=c):
                if c > 0:
                    kb.emit(act, lambda e, j=j: e.copy(xnh[j][:, 0:2], xnh[j][:, CH:CH + 2]),
                            reads=(t_xnh[j],), writes=(t_xnh[j],))
                    kb.emit(pool, lambda e, j=j: e.tensor_copy(xnl[j][:, 0:2], xnl[j][:, CH:CH + 2]),
                            reads=(t_xnl[j],), writes=(t_xnl[j],))
                grp = []
                for i in range(NTT):
                    def fn(e, j=j, i=i, first=(i == 0), last_=(i == NTT - 1)):
                        return e.matmul(P[5].ap()[:, i * 128:(i + 1) * 128],
                                        x_tm[i][:, j * 128:(j + 1) * 128], ident_sb[:],
                                        start=first, stop=last_, is_transpose=True,
                                        skip_group_check=True)
                    grp.append((fn, [t_x_tm[i], t_consts]))
                kb.mm_group(grp, (t_P[5],))
                split_cols(lambda: P[5].ap()[:, 0:CH],
                           xnh[j][:, 2:2 + CH], xnl[j][:, 2:2 + CH],
                           (t_P[5],), t_xnh[j], t_xnl[j])
            ps.append(p)

        def p_msub():
            for i in range(NTT):
                cb8 = 8 * i
                kb.emit(dve, lambda e, i=i, cb8=cb8: e.tensor_scalar(
                    x_tm[i][:], x_tm[i][:], ln_sc.ap()[:, cb8 + 5:cb8 + 6], None, ALU.add),
                    reads=(t_lnsc, t_x_tm[i]), writes=(t_x_tm[i],))
        ps.append(p_msub)

        for j in range(NK):
            def p(j=j):
                grp = []
                for i in range(NTT):
                    def fn(e, j=j, i=i, first=(i == 0), last_=(i == NTT - 1)):
                        return e.matmul(P[5].ap()[:, i * 128:(i + 1) * 128],
                                        x_tm[i][:, j * 128:(j + 1) * 128], ident_sb[:],
                                        start=first, stop=last_, is_transpose=True,
                                        skip_group_check=True)
                    grp.append((fn, [t_x_tm[i], t_consts]))
                kb.mm_group(grp, (t_P[5],))
                split_cols(lambda: P[5].ap()[:, 0:CH],
                           xih[j][:], xil[j][:],
                           (t_P[5],), t_xih[j], t_xil[j])
            ps.append(p)
        return ps

    def pieces_xz_x(n):
        ps = []
        for blk in range(4):
            F0 = blk * 4
            st = {}

            def p_k(k, F0=F0, st=st):
                if 'ma' not in st:
                    st['ma'] = MultiAcc(kb, t_P)
                ma = st['ma']
                slot = k % 2
                kb.dma(sp, sem_wh[slot],
                       lambda e, k=k, F0=F0, slot=slot: e.dma_start(
                           out=wb_h[slot][:],
                           in_=win_h[k * 128:(k + 1) * 128, F0 * 128:F0 * 128 + 512]),
                       reads=(t_w_win,), writes=(t_wbh[slot],))
                kb.dma(sp, sem_wl[slot],
                       lambda e, k=k, F0=F0, slot=slot: e.dma_start(
                           out=wb_l[slot][:],
                           in_=win_l[k * 128:(k + 1) * 128, F0 * 128:F0 * 128 + 512]),
                       reads=(t_w_win,), writes=(t_wbl[slot],))
                for fi in range(4):
                    mm_hl3(ma, fi,
                           wb_h[slot][:, fi * 128:(fi + 1) * 128],
                           wb_l[slot][:, fi * 128:(fi + 1) * 128],
                           xi_hl[k], CH, 0, CH,
                           (t_wbh[slot],), (t_wbl[slot],), (t_xih[k], t_xil[k]),
                           first=(k == 0), last=(k == NK - 1))
            for k in range(NK):
                ps.append(lambda k=k, p_k=p_k: p_k(k))

            def p_evac(F0=F0, st=st):
                st['ma'].finish()
                for fi in range(4):
                    jj = F0 + fi
                    evac_sum2(dve, lambda jj=jj: x_pre[jj][:, 3:3 + CH], fi, CH,
                              writes=(t_xpre[jj],))
            ps.append(p_evac)
        return ps

    def pieces_z(n):
        b, c, s0, colsS, base2, last = params(n)
        ps = []
        for blk in range(4):
            F0 = NJ + blk * 4
            st = {}

            def p_k(k, F0=F0, st=st):
                if 'ma' not in st:
                    st['ma'] = MultiAcc(kb, t_P)
                ma = st['ma']
                slot = k % 2
                kb.dma(sp, sem_wh[slot],
                       lambda e, k=k, F0=F0, slot=slot: e.dma_start(
                           out=wb_h[slot][:],
                           in_=win_h[k * 128:(k + 1) * 128, F0 * 128:F0 * 128 + 512]),
                       reads=(t_w_win,), writes=(t_wbh[slot],))
                kb.dma(sp, sem_wl[slot],
                       lambda e, k=k, F0=F0, slot=slot: e.dma_start(
                           out=wb_l[slot][:],
                           in_=win_l[k * 128:(k + 1) * 128, F0 * 128:F0 * 128 + 512]),
                       reads=(t_w_win,), writes=(t_wbl[slot],))
                for fi in range(4):
                    mm_hl3(ma, fi,
                           wb_h[slot][:, fi * 128:(fi + 1) * 128],
                           wb_l[slot][:, fi * 128:(fi + 1) * 128],
                           xi_hl[k], CH, 0, CH,
                           (t_wbh[slot],), (t_wbl[slot],), (t_xih[k], t_xil[k]),
                           first=(k == 0), last=(k == NK - 1))
            for k in range(NK):
                ps.append(lambda k=k, p_k=p_k: p_k(k))

            def p_evac(F0=F0, st=st, c=c):
                st['ma'].finish()
                for fi in range(4):
                    jj = F0 - NJ + fi
                    if c > 0:
                        kb.emit(act, lambda e, jj=jj: e.copy(sz_buf[jj][:, 0:2],
                                                             sz_buf[jj][:, CH:CH + 2]),
                                reads=(t_sz[jj],), writes=(t_sz[jj],))
                    stg2, tstg2 = conv_acc[fi % 2], t_cacc[fi % 2]
                    evac_sum2(dve, lambda stg2=stg2: stg2.ap()[:, 0:CH], fi, CH,
                              writes=(tstg2,))
                    kb.emit(act, lambda e, jj=jj, stg2=stg2: e.activation(
                        sz_buf[jj][:, 2:2 + CH], stg2.ap()[:, 0:CH], AF.Silu,
                        bias=0.0, scale=1.0),
                        reads=(tstg2,), writes=(t_sz[jj],))
            ps.append(p_evac)
        return ps

    def pieces_conv(n):
        b, c, s0, colsS, base2, last = params(n)
        off0 = base2
        ps = []
        if c == 0:
            ps.append(lambda: kb.emit(dve, lambda e: e.memset(tails.ap(), 0.0),
                                      reads=(), writes=(t_tails,)))
        for jj in range(NJ):
            def p(jj=jj, off0=off0, colsS=colsS, last=last):
                kb.emit(act, lambda e, jj=jj: e.copy(x_pre[jj][:, 0:3],
                                                     tails.ap()[:, jj * 3:jj * 3 + 3]),
                        reads=(t_tails,), writes=(t_xpre[jj],))
                if last:
                    kb.emit(dve, lambda e, jj=jj: e.memset(x_pre[jj][:, 3 + CH:5 + CH], 0.0),
                            reads=(), writes=(t_xpre[jj],))
                acc, tacc = conv_acc[jj % 2], t_cacc[jj % 2]
                kb.emit(dve, lambda e, jj=jj, acc=acc, off0=off0, colsS=colsS:
                        e.tensor_scalar(
                            acc.ap()[:, 0:colsS],
                            x_pre[jj][:, off0: off0 + colsS],
                            cw_sb[:, jj * 4: jj * 4 + 1], None, ALU.mult),
                        reads=(t_xpre[jj], t_consts), writes=(tacc,))
                for k4 in range(1, 4):
                    kb.emit(dve, lambda e, jj=jj, acc=acc, k4=k4, off0=off0, colsS=colsS:
                            e.scalar_tensor_tensor(
                                acc.ap()[:, 0:colsS],
                                x_pre[jj][:, off0 + k4: off0 + k4 + colsS],
                                cw_sb[:, jj * 4 + k4: jj * 4 + k4 + 1],
                                acc.ap()[:, 0:colsS], ALU.mult, ALU.add),
                            reads=(t_xpre[jj], t_consts, tacc), writes=(tacc,))
                if not last:
                    kb.emit(act, lambda e, jj=jj: e.copy(tails.ap()[:, jj * 3:jj * 3 + 3],
                                                         x_pre[jj][:, CH:CH + 3]),
                            reads=(t_xpre[jj],), writes=(t_tails,))
                kb.emit(act, lambda e, jj=jj, acc=acc, colsS=colsS: e.activation(
                    x_pre[jj][:, 0:colsS], acc.ap()[:, 0:colsS], AF.Silu,
                    bias=cb_sb[:, jj:jj + 1], scale=1.0),
                    reads=(tacc, t_consts), writes=(t_xpre[jj],))
                split_cols(lambda jj=jj, colsS=colsS: x_pre[jj][:, 0:colsS],
                           xfh[jj][:, 0:colsS], xfl[jj][:, 0:colsS],
                           (t_xpre[jj],), t_xfh[jj], t_xfl[jj])
            ps.append(p)
        return ps

    def pieces_delta(n):
        b, c, s0, colsS, base2, last = params(n)
        blks = []
        for blk in range(4):
            j0 = blk * 4
            with_wx = blk == 0
            st = {}
            ps = []

            def p_k(k, j0=j0, st=st, colsS=colsS, with_wx=with_wx):
                if 'ma' not in st:
                    st['ma'] = MultiAcc(kb, t_P)
                ma = st['ma']
                slot = k % 2
                kb.dma(sp, sem_wh[slot],
                       lambda e, k=k, slot=slot, j0=j0: e.dma_start(
                           out=wb_h[slot][:],
                           in_=wdt_h[k * 128:(k + 1) * 128, j0 * 128:j0 * 128 + 512]),
                       reads=(t_w_wdt,), writes=(t_wbh[slot],))
                kb.dma(sp, sem_wl[slot],
                       lambda e, k=k, slot=slot, j0=j0: e.dma_start(
                           out=wb_l[slot][:],
                           in_=wdt_l[k * 128:(k + 1) * 128, j0 * 128:j0 * 128 + 512]),
                       reads=(t_w_wdt,), writes=(t_wbl[slot],))
                for ji in range(4):
                    mm_hl3(ma, ji,
                           wb_h[slot][:, ji * 128:(ji + 1) * 128],
                           wb_l[slot][:, ji * 128:(ji + 1) * 128],
                           xf_hl[k], CS_MAX, 0, colsS,
                           (t_wbh[slot],), (t_wbl[slot],), (t_xfh[k], t_xfl[k]),
                           first=(k == 0), last=(k == NJ - 1))
                if with_wx:
                    mm_hl3(ma, 4,
                           wx_h[:, k * 2 * DS:(k + 1) * 2 * DS],
                           wx_l[:, k * 2 * DS:(k + 1) * 2 * DS],
                           xf_hl[k], CS_MAX, 0, colsS,
                           (t_wx,), (t_wx,), (t_xfh[k], t_xfl[k]),
                           first=(k == 0), last=(k == NJ - 1), rows=32)
            for k in range(NJ):
                ps.append(lambda k=k, p_k=p_k: p_k(k))

            def p_evac(j0=j0, st=st, colsS=colsS, with_wx=with_wx):
                st['ma'].finish()
                for ji in range(4):
                    j = j0 + ji
                    evac_sum2(dve, lambda j=j, colsS=colsS: delta[j][:, 0:colsS],
                              ji, colsS, writes=(t_dl[j],))
                if with_wx:
                    if 2 * colsS <= 512:
                        kb.emit(dve, lambda e, colsS=colsS: e.tensor_reduce(
                            bc_sb[0:32, 0:colsS], psum2_ap(4, colsS, rows=32),
                            AX.X, ALU.add),
                            reads=(t_pacc[4],), writes=(t_bc,))
                    else:
                        kb.emit(dve, lambda e, colsS=colsS: e.tensor_copy(
                            bc_sb[0:32, 0:colsS], pacc(4)[0:32, 0:colsS]),
                            reads=(t_pacc[4],), writes=(t_bc,))
            ps.append(p_evac)
            blks.append(ps)
        return blks

    def em_post_q(n, qb):
        b, c, s0, colsS, base2, last = params(n)
        js = list(range(qb * 4, (qb + 1) * 4))
        for j in js:
            kb.emit(act, lambda e, j=j, colsS=colsS: e.activation(
                wv[j][:, 0:colsS], delta[j][:, 0:colsS], AF.Sigmoid,
                bias=bdt_sb[:, j:j + 1], scale=1.0),
                reads=(t_dl[j], t_consts), writes=(t_wv[j],))
        for j in js:
            kb.emit(act, lambda e, j=j, colsS=colsS: e.activation(
                wv[j][:, 0:colsS], wv[j][:, 0:colsS], AF.Ln, bias=0.0, scale=1.0),
                reads=(t_wv[j],), writes=(t_wv[j],))
        for j in js:
            kb.emit(dve, lambda e, j=j, colsS=colsS: e.scalar_tensor_tensor(
                delta[j][:, 0:colsS], delta[j][:, 0:colsS],
                bdt_sb[:, j:j + 1], wv[j][:, 0:colsS],
                ALU.add, ALU.subtract),
                reads=(t_wv[j], t_dl[j], t_consts), writes=(t_dl[j],))
        for j in js:
            kb.emit(pool, lambda e, j=j, colsS=colsS: e.tensor_add(
                y_t[j][:, 0:colsS], xfh[j][:, 0:colsS], xfl[j][:, 0:colsS]),
                reads=(t_xfh[j], t_xfl[j]), writes=(t_y[j],))
            kb.emit(pool, lambda e, j=j, colsS=colsS: e.tensor_mul(
                wv[j][:, 0:colsS], delta[j][:, 0:colsS], y_t[j][:, 0:colsS]),
                reads=(t_dl[j], t_y[j]), writes=(t_wv[j],))
            kb.emit(act, lambda e, j=j, colsS=colsS: e.activation(
                y_t[j][:, 0:colsS], y_t[j][:, 0:colsS], AF.Identity,
                bias=0.0, scale=dcol_sb[:, j:j + 1]),
                reads=(t_y[j], t_consts), writes=(t_y[j],))

    def em_prefill(n):
        b, c, s0, colsS, base2, last = params(n)
        par = n % 2
        for o in range(NK):
            kb.emit(pool, lambda e, o=o, par=par, base2=base2, colsS=colsS: e.tensor_add(
                s_sb[par][o][:, 0:colsS],
                xnh[o][:, base2:base2 + colsS], xnl[o][:, base2:base2 + colsS]),
                reads=(t_xnh[o], t_xnl[o]), writes=(t_s[par][o],))
        if c == 0:
            kb.emit(dve, lambda e: e.memset(carries.ap(), 0.0), reads=(),
                    writes=tuple(t for row in t_car for t in row))

    BCBANKS = (4, 6, 7)

    def em_bcast(n, jb, s):
        b, c, s0, colsS, base2, last = params(n)
        si = jb * DS + s
        rb = si % 2
        sb_i = si % NBREP
        bk = BCBANKS[si % 3]
        kb.dma(sp, sem_row[rb],
               lambda e, s=s, rb=rb, colsS=colsS: e.dma_start(
                   out=brow[rb][0:1, 0:colsS], in_=bc_sb[s:s + 1, 0:colsS]),
               reads=(t_bc,), writes=(t_brow[rb],))
        if 2 * colsS <= 512:
            kb.dma(sp, sem_row[rb],
                   lambda e, s=s, rb=rb, colsS=colsS: e.dma_start(
                       out=brow[rb][0:1, colsS:2 * colsS],
                       in_=bc_sb[DS + s:DS + s + 1, 0:colsS]),
                   reads=(t_bc,), writes=(t_brow[rb],))
            kb.emit(pe, lambda e, rb=rb, bk=bk, colsS=colsS: e.matmul(
                P[bk].ap()[:, 0:2 * colsS], ones_sb[:], brow[rb][0:1, 0:2 * colsS],
                start=True, stop=True),
                reads=(t_brow[rb], t_consts), writes=(t_P[bk],))
            kb.emit(act, lambda e, sb_i=sb_i, bk=bk, colsS=colsS: e.copy(
                bcrep[sb_i][:, 0:2 * colsS], P[bk].ap()[:, 0:2 * colsS]),
                reads=(t_P[bk],), writes=(t_bcr[sb_i],))
        else:
            kb.emit(pe, lambda e, rb=rb, bk=bk, colsS=colsS: e.matmul(
                P[bk].ap()[:, 0:colsS], ones_sb[:], brow[rb][0:1, 0:colsS],
                start=True, stop=True),
                reads=(t_brow[rb], t_consts), writes=(t_P[bk],))
            kb.emit(act, lambda e, sb_i=sb_i, bk=bk, colsS=colsS: e.copy(
                bcrep[sb_i][:, 0:colsS], P[bk].ap()[:, 0:colsS]),
                reads=(t_P[bk],), writes=(t_bcr[sb_i],))
            kb.dma(sp, sem_row[rb],
                   lambda e, s=s, rb=rb, colsS=colsS: e.dma_start(
                       out=brow[rb][0:1, CS_MAX:CS_MAX + colsS],
                       in_=bc_sb[DS + s:DS + s + 1, 0:colsS]),
                   reads=(t_bc,), writes=(t_brow[rb],))
            kb.emit(pe, lambda e, rb=rb, bk=bk, colsS=colsS: e.matmul(
                P[bk].ap()[:, 0:colsS], ones_sb[:],
                brow[rb][0:1, CS_MAX:CS_MAX + colsS],
                start=True, stop=True),
                reads=(t_brow[rb], t_consts, t_bcr[sb_i]), writes=(t_P[bk],))
            kb.emit(act, lambda e, sb_i=sb_i, bk=bk, colsS=colsS: e.copy(
                bcrep[sb_i][:, CS_MAX:CS_MAX + colsS], P[bk].ap()[:, 0:colsS]),
                reads=(t_P[bk],), writes=(t_bcr[sb_i],))

    def em_extract(n, jb, s):
        b, c, s0, colsS, base2, last = params(n)

        def carry_fn(e, s=s, jb=jb, colsS=colsS):
            hv = h_rg.ap().rearrange("p (j cc) -> p j cc", cc=CS_MAX)
            src = hv[:, 0:4, colsS - 1:colsS]
            cv = carries.ap().rearrange("p (j s2) -> p j s2", s2=DS)
            dst = cv[:, jb * 4:(jb + 1) * 4, s:s + 1]
            return e.tensor_copy(dst, src)
        kb.emit(pool, carry_fn, reads=tuple(t_h), writes=(t_car[s][jb],))

    def em_scan_hs(n, hb, s, pend):
        b, c, s0, colsS, base2, last = params(n)
        sb_i = (hb * DS + s) % NBREP
        coff = colsS if 2 * colsS <= 512 else CS_MAX
        for gi in range(2):
            g4 = 2 * hb + gi
            js = list(range(g4 * 4, (g4 + 1) * 4))
            for j in js:
                ai = j % NRING
                kb.emit(act, lambda e, j=j, s=s, ai=ai, colsS=colsS: e.activation(
                    a_t[ai][:, 0:colsS], delta[j][:, 0:colsS], AF.Exp,
                    bias=0.0, scale=acol_sb[:, j * DS + s: j * DS + s + 1]),
                    reads=(t_dl[j], t_consts), writes=(t_a[ai],))
            if pend[0] is not None:
                em_extract(n, pend[0][0], pend[0][1])
            pend[0] = (g4, s)
            for j in js:
                ai = j % NRING
                kb.emit(dve, lambda e, j=j, ai=ai, sb_i=sb_i, colsS=colsS: e.tensor_mul(
                    b_t[ai][:, 0:colsS], wv[j][:, 0:colsS], bcrep[sb_i][:, 0:colsS]),
                    reads=(t_wv[j], t_bcr[sb_i]), writes=(t_b[ai],))
            for j in js:
                ai = j % NRING
                cidx = j * DS + s
                kb.emit(dve, lambda e, j=j, ai=ai, cidx=cidx, colsS=colsS:
                        e.tensor_tensor_scan(
                            hslot(j)[:, 0:colsS],
                            a_t[ai][:, 0:colsS], b_t[ai][:, 0:colsS],
                            carries.ap()[:, cidx:cidx + 1], ALU.mult, ALU.add),
                        reads=(t_a[ai], t_b[ai], t_car[s][g4]), writes=(t_h[ai],))
            for j in js:
                ai = j % NRING
                ceng = dve if j % 2 == 0 else pool
                yeng = ceng
                ci = j % 2
                kb.emit(ceng, lambda e, j=j, ci=ci, sb_i=sb_i, coff=coff, colsS=colsS:
                        e.tensor_mul(
                            cm_t[ci][:, 0:colsS],
                            hslot(j)[:, 0:colsS],
                            bcrep[sb_i][:, coff:coff + colsS]),
                        reads=(t_h[ai], t_bcr[sb_i]), writes=(t_cm[ci],))
                kb.emit(yeng, lambda e, j=j, ci=ci, colsS=colsS: e.tensor_add(
                    y_t[j][:, 0:colsS], y_t[j][:, 0:colsS], cm_t[ci][:, 0:colsS]),
                    reads=(t_cm[ci], t_y[j]), writes=(t_y[j],))

    def em_yfin(n):
        b, c, s0, colsS, base2, last = params(n)
        for j in range(NJ):
            kb.emit(dve, lambda e, j=j, base2=base2, colsS=colsS: e.tensor_mul(
                y_t[j][:, 0:colsS], y_t[j][:, 0:colsS],
                sz_buf[j][:, base2:base2 + colsS]),
                reads=(t_sz[j], t_y[j]), writes=(t_y[j],))
            split_cols(lambda j=j, colsS=colsS: y_t[j][:, 0:colsS],
                       ybh[j][:, 0:colsS], ybl[j][:, 0:colsS],
                       (t_y[j],), t_ybh[j], t_ybl[j])

    def pieces_wout(n):
        b, c, s0, colsS, base2, last = params(n)
        par = n % 2
        ps = []
        for blk in range(2):
            o0 = blk * 4
            st = {}

            def p_k(k, o0=o0, st=st, colsS=colsS):
                if 'ma' not in st:
                    st['ma'] = MultiAcc(kb, t_P)
                ma = st['ma']
                slot = k % 2
                kb.dma(sp, sem_wh[slot],
                       lambda e, k=k, slot=slot, o0=o0: e.dma_start(
                           out=wb_h[slot][:],
                           in_=wout_h[k * 128:(k + 1) * 128, o0 * 128:o0 * 128 + 512]),
                       reads=(t_w_wout,), writes=(t_wbh[slot],))
                kb.dma(sp, sem_wl[slot],
                       lambda e, k=k, slot=slot, o0=o0: e.dma_start(
                           out=wb_l[slot][:],
                           in_=wout_l[k * 128:(k + 1) * 128, o0 * 128:o0 * 128 + 512]),
                       reads=(t_w_wout,), writes=(t_wbl[slot],))
                for oi in range(4):
                    mm_hl3(ma, oi,
                           wb_h[slot][:, oi * 128:(oi + 1) * 128],
                           wb_l[slot][:, oi * 128:(oi + 1) * 128],
                           y_bf[k], CS_MAX, 0, colsS,
                           (t_wbh[slot],), (t_wbl[slot],), (t_ybh[k], t_ybl[k]),
                           first=(k == 0), last=(k == NJ - 1))
            for k in range(NJ):
                ps.append(lambda k=k, p_k=p_k: p_k(k))

            def p_evac(o0=o0, st=st, colsS=colsS, par=par):
                st['ma'].finish()
                for oi in range(4):
                    evac_add2(o0 + oi, par, oi, colsS)
            ps.append(p_evac)
        return ps

    def pieces_prodout(n):
        b, c, s0, colsS, base2, last = params(n)
        par = n % 2
        ps = []

        def p_prod():
            for o in range(NK):
                kb.emit(pool, lambda e, o=o, par=par, colsS=colsS: e.tensor_mul(
                    s_sb[par][o][:, 0:colsS], s_sb[par][o][:, 0:colsS],
                    g_sb[o][:, 0:colsS]),
                    reads=(t_g[o], t_s[par][o]), writes=(t_s[par][o],))
        ps.append(p_prod)

        nsub = (colsS + 127) // 128
        for ti in range(nsub):
            def p_sub(ti=ti, s0=s0, colsS=colsS, par=par, b=b):
                u0 = ti * 128
                Fz = min(128, colsS - u0)
                for rnd in range(2):
                    grp = []
                    for oi in range(4):
                        o = rnd * 4 + oi
                        def fn(e, o=o, oi=oi, u0=u0, Fz=Fz, par=par,
                               first=(oi == 0), last_=(oi == 3)):
                            return e.matmul(P[5].ap()[0:Fz, oi * 128:(oi + 1) * 128],
                                            s_sb[par][o][:, u0:u0 + Fz], ident_sb[:],
                                            start=first, stop=last_, is_transpose=True,
                                            skip_group_check=True)
                        grp.append((fn, [t_s[par][o], t_consts]))
                    kb.mm_group(grp, (t_P[5],))
                    kb.emit(act, lambda e, rnd=rnd, Fz=Fz: e.copy(
                        ostg.ap()[0:Fz, rnd * 512:(rnd + 1) * 512], P[5].ap()[0:Fz, :]),
                        reads=(t_P[5],), writes=(t_ostg,))
                tok0 = s0 + u0
                kb.dma(pool, sem_oacc,
                       lambda e, Fz=Fz, tok0=tok0, b=b: e.dma_start(
                           out=out[b, tok0:tok0 + Fz, :], in_=ostg[0:Fz, :],
                           accum_op=ALU.add),
                       reads=(t_ostg,), writes=(t_out_hbm,))
            ps.append(p_sub)
        return ps

    def pieces_ffn(n):
        b, c, s0, colsS, base2, last = params(n)
        ps = []
        for blk in range(2):
            o0 = blk * 4
            st = {}

            def p_k(k, o0=o0, st=st, colsS=colsS, base2=base2):
                if 'ma' not in st:
                    st['ma'] = MultiAcc(kb, t_P)
                ma = st['ma']
                slot = k % 2
                kb.dma(sp, sem_wh[slot],
                       lambda e, k=k, slot=slot, o0=o0: e.dma_start(
                           out=wb_h[slot][:],
                           in_=wtT_scr[k * 128:(k + 1) * 128, o0 * 128:o0 * 128 + 512]),
                       reads=(t_w_ffn,), writes=(t_wbh[slot],))
                for oi in range(4):
                    mm_hl1(ma, oi,
                           wb_h[slot][:, oi * 128:(oi + 1) * 128],
                           xn_hl[k], XNW, base2, colsS,
                           (t_wbh[slot],), (t_xnh[k], t_xnl[k]),
                           first=(k == 0), last=(k == NK - 1))
            for k in range(NK):
                ps.append(lambda k=k, p_k=p_k: p_k(k))

            def p_evac(o0=o0, st=st, colsS=colsS):
                st['ma'].finish()
                for oi in range(4):
                    o = o0 + oi
                    if 2 * colsS <= 512:
                        stg2, tstg2 = conv_acc[oi % 2], t_cacc[oi % 2]
                        evac_sum2(dve, lambda stg2=stg2, colsS=colsS: stg2.ap()[:, 0:colsS],
                                  oi, colsS, writes=(tstg2,))
                        kb.emit(act, lambda e, o=o, stg2=stg2, colsS=colsS: e.activation(
                            g_sb[o][:, 0:colsS], stg2.ap()[:, 0:colsS], AF.Silu,
                            bias=ffnb_sb[:, o:o + 1], scale=alpha_sb[:, 0:1]),
                            reads=(tstg2, t_consts), writes=(t_g[o],))
                    else:
                        kb.emit(act, lambda e, o=o, oi=oi, colsS=colsS: e.activation(
                            g_sb[o][:, 0:colsS], pacc(oi)[:, 0:colsS], AF.Silu,
                            bias=ffnb_sb[:, o:o + 1], scale=alpha_sb[:, 0:1]),
                            reads=(t_pacc[oi], t_consts), writes=(t_g[o],))
            ps.append(p_evac)
        return ps

    NQ = 2 * DS

    def scan_window(n, ps):
        rem = list(ps)
        pend = [None]
        LOOK = 3
        for si in range(NQ):
            hb, s = divmod(si, DS)
            if si == 0:
                for la in range(LOOK):
                    em_bcast(n, la // DS, la % DS)
            em_scan_hs(n, hb, s, pend)
            if si + LOOK < NQ:
                em_bcast(n, (si + LOOK) // DS, (si + LOOK) % DS)
            if rem:
                q = -(-len(rem) // max(1, NQ - 7 - si))
                for _ in range(min(q, len(rem))):
                    rem.pop(0)()
        em_extract(n, pend[0][0], pend[0][1])
        while rem:
            rem.pop(0)()

    for p in (pieces_input_ln(0) + pieces_transposes(0) + pieces_xz_x(0)
              + pieces_conv(0)):
        p()

    for n in range(NSTEP):
        if n > 0:
            em_yfin(n - 1)
        dblks = pieces_delta(n)
        for p in dblks[0] + dblks[1]:
            p()
        em_post_q(n, 0)
        em_post_q(n, 1)
        em_prefill(n)
        ps = []
        for qb in (2, 3):
            ps += dblks[qb]
            ps.append(lambda n=n, qb=qb: em_post_q(n, qb))
        ps += pieces_z(n)
        if n > 0:
            ps += pieces_wout(n - 1) + pieces_prodout(n - 1)
        ps += pieces_ffn(n)
        if n + 1 < NSTEP:
            ps += (pieces_input_ln(n + 1) + pieces_transposes(n + 1)
                   + pieces_xz_x(n + 1) + pieces_conv(n + 1))
        scan_window(n, ps)

    em_yfin(NSTEP - 1)
    for p in pieces_wout(NSTEP - 1) + pieces_prodout(NSTEP - 1):
        p()

    def final(sync):
        sync.wait_ge(sem_bulk, kb.dma_counts[id(sem_bulk)])
        if kb.dma_counts[id(sem_oacc)]:
            sync.wait_ge(sem_oacc, kb.dma_counts[id(sem_oacc)])
    kb.sp.thunks.append(final)

    with nc.Block() as block:
        @block.sync
        def _(sync):
            for th in kb.sp.thunks:
                th(sync)

        @block.tensor
        def _(te):
            for th in kb.pe.thunks:
                th(te)

        @block.vector
        def _(v):
            for th in kb.dve.thunks:
                th(v)

        @block.scalar
        def _(sc):
            for th in kb.act.thunks:
                th(sc)

        @block.gpsimd
        def _(g):
            for th in kb.pool.thunks:
                th(g)

    return nc



def make_in_map(inp_shard, W):
    a_cols = np.ascontiguousarray(
        np.asarray(W["L0_attn_A"]).reshape(NJ, 128, DS).transpose(1, 0, 2).reshape(128, -1))
    d_cols = np.ascontiguousarray(np.asarray(W["L0_attn_D"]).reshape(NJ, 128).T)
    convw = np.asarray(W["L0_attn_conv_w"])[:, 0, :]
    convw_cols = np.ascontiguousarray(convw.reshape(4, NJ, 128).transpose(2, 1, 0).reshape(128, -1))
    convb_cols = np.ascontiguousarray(np.asarray(W["L0_attn_conv_b"]).reshape(NJ, 128).T)
    bdt_cols = np.ascontiguousarray(np.asarray(W["L0_attn_b_dt"]).reshape(NJ, 128).T)
    ffnb_cols = np.ascontiguousarray(np.asarray(W["L0_ffn_bias"]).reshape(NK, 128).T)
    alpha_bc = np.ascontiguousarray(
        np.broadcast_to(np.asarray(W["L0_ffn_alpha"]).reshape(1, 1), (128, 1)))
    return {
        "inp": np.ascontiguousarray(inp_shard, dtype=np.float32),
        "w_in": np.ascontiguousarray(np.asarray(W["L0_attn_W_in"], dtype=np.float32)),
        "w_x": np.ascontiguousarray(np.asarray(W["L0_attn_W_x"], dtype=np.float32)),
        "w_dt": np.ascontiguousarray(np.asarray(W["L0_attn_W_dt"], dtype=np.float32)),
        "w_out": np.ascontiguousarray(np.asarray(W["L0_attn_W_out"], dtype=np.float32)),
        "ffn_w": np.ascontiguousarray(np.asarray(W["L0_ffn_weight"], dtype=np.float32)),
        "a_cols": a_cols.astype(np.float32),
        "d_cols": d_cols.astype(np.float32),
        "convw_cols": convw_cols.astype(np.float32),
        "convb_cols": convb_cols.astype(np.float32),
        "bdt_cols": bdt_cols.astype(np.float32),
        "ffnb_cols": ffnb_cols.astype(np.float32),
        "alpha_bc": alpha_bc.astype(np.float32),
        "ident_in": np.eye(128, dtype=np.float32),
        "ones_in": np.ones((1, 128), dtype=np.float32),
        "onescol_in": np.ones((128, 1), dtype=np.float32),
    }


def kernel(**inputs):
    full = inputs["L0_attn_inp"]
    BATCH, T = full.shape[0], full.shape[1]
    B = BATCH // NCORES
    nc = build_nc(B, T, 256)
    in_maps = [make_in_map(full[ci * B:(ci + 1) * B], inputs) for ci in range(NCORES)]
    res = run_bass_kernel_spmd(nc, in_maps, core_ids=list(range(NCORES)))
    return np.concatenate([np.asarray(r["out"]) for r in res.results], axis=0)


# version 29
# speedup vs baseline: 1.0537x; 1.0537x over previous
import numpy as np

import concourse.bass as bass
import concourse.mybir as mybir
from concourse.bass_utils import run_bass_kernel_spmd

F32 = mybir.dt.float32
BF16 = mybir.dt.bfloat16
AF = mybir.ActivationFunctionType
ALU = mybir.AluOpType
AX = mybir.AxisListType

D = 1024
DI = 2048
DS = 16
NCORES = 8
NJ = DI // 128
NK = D // 128


class Ev:
    __slots__ = ("sem", "val")

    def __init__(self, sem, val):
        self.sem = sem
        self.val = val


class TT:
    __slots__ = ("name", "w", "rs")

    def __init__(self, name):
        self.name = name
        self.w = None
        self.rs = []


class Rec:
    def __init__(self, sem):
        self.sem = sem
        self.n = 0
        self.marks = {}
        self.thunks = []

    def wait(self, ev):
        if ev is None:
            return
        key = id(ev.sem)
        if self.marks.get(key, 0) >= ev.val:
            return
        self.marks[key] = ev.val
        s, v = ev.sem, ev.val
        self.thunks.append(lambda e: e.wait_ge(s, v))

    def waits(self, deps):
        for d in deps:
            self.wait(d)

    def op(self, fn, deps=()):
        self.waits(deps)
        self.n += 1
        mysem, myval = self.sem, self.n
        self.thunks.append(lambda e: fn(e).then_inc(mysem, 1))
        return Ev(mysem, myval)


class KB:
    def __init__(self, nc):
        self.nc = nc
        self.dma_counts = {}
        self.pe = Rec(nc.alloc_semaphore("sem_pe"))
        self.dve = Rec(nc.alloc_semaphore("sem_dve"))
        self.act = Rec(nc.alloc_semaphore("sem_act"))
        self.pool = Rec(nc.alloc_semaphore("sem_pool"))
        self.sp = Rec(nc.alloc_semaphore("sem_sp"))

    def dma_sem(self, name):
        s = self.nc.alloc_semaphore(name)
        self.dma_counts[id(s)] = 0
        return s

    @staticmethod
    def _deps(reads, writes):
        deps = []
        for t in reads:
            if t.w is not None:
                deps.append(t.w)
        for t in writes:
            deps.extend(t.rs)
            if t.w is not None:
                deps.append(t.w)
        return deps

    @staticmethod
    def _record(ev, reads, writes):
        for t in reads:
            t.rs.append(ev)
        for t in writes:
            t.w = ev
            t.rs = []
        return ev

    def emit(self, eng, fn, reads=(), writes=()):
        ev = eng.op(fn, self._deps(reads, writes))
        return self._record(ev, reads, writes)

    def dma(self, eng, sem, fn, reads=(), writes=()):
        eng.waits(self._deps(reads, writes))
        self.dma_counts[id(sem)] += 16
        cnt = self.dma_counts[id(sem)]
        eng.thunks.append(lambda e: fn(e).then_inc(sem, 16))
        ev = Ev(sem, cnt)
        return self._record(ev, reads, writes)

    def mm_group(self, items, psum_writes):
        pe = self.pe
        first_deps = []
        for t in psum_writes:
            first_deps.extend(t.rs)
            if t.w is not None:
                first_deps.append(t.w)
        ev = None
        for i, (fn, rts) in enumerate(items):
            deps = list(first_deps) if i == 0 else []
            for t in rts:
                if t.w is not None:
                    deps.append(t.w)
            ev = pe.op(fn, deps)
            for t in rts:
                t.rs.append(ev)
        for t in psum_writes:
            t.w = ev
            t.rs = []
        return ev


class MultiAcc:

    def __init__(self, kb, tiles):
        self.kb = kb
        self.tiles = tiles
        self.started = set()
        self.evs = {}

    def mm(self, fn, reads, gi):
        deps = []
        if gi not in self.started:
            t = self.tiles[gi]
            deps.extend(t.rs)
            if t.w is not None:
                deps.append(t.w)
            self.started.add(gi)
        for t in reads:
            if t.w is not None:
                deps.append(t.w)
        ev = self.kb.pe.op(fn, deps)
        for t in reads:
            t.rs.append(ev)
        self.evs[gi] = ev
        return ev

    def finish(self):
        for gi in self.started:
            t = self.tiles[gi]
            t.w = self.evs[gi]
            t.rs = []



def build_nc(B, T, CH):
    NC_STEPS = T // CH
    NTT = CH // 128
    CS_MAX = CH + 2
    NSTEP = B * NC_STEPS
    nc = bass.Bass("TRN2", target_bir_lowering=False)
    kb = KB(nc)

    inp = nc.dram_tensor("inp", [B, T, D], F32, kind="ExternalInput").ap()
    w_in = nc.dram_tensor("w_in", [D, 2 * DI], F32, kind="ExternalInput").ap()
    w_x = nc.dram_tensor("w_x", [DI, 2 * DS], F32, kind="ExternalInput").ap()
    w_dt = nc.dram_tensor("w_dt", [DI, DI], F32, kind="ExternalInput").ap()
    w_out = nc.dram_tensor("w_out", [DI, D], F32, kind="ExternalInput").ap()
    ffn_w = nc.dram_tensor("ffn_w", [D, D], F32, kind="ExternalInput").ap()
    a_cols = nc.dram_tensor("a_cols", [128, NJ * DS], F32, kind="ExternalInput").ap()
    d_cols = nc.dram_tensor("d_cols", [128, NJ], F32, kind="ExternalInput").ap()
    convw_cols = nc.dram_tensor("convw_cols", [128, NJ * 4], F32, kind="ExternalInput").ap()
    convb_cols = nc.dram_tensor("convb_cols", [128, NJ], F32, kind="ExternalInput").ap()
    bdt_cols = nc.dram_tensor("bdt_cols", [128, NJ], F32, kind="ExternalInput").ap()
    ffnb_cols = nc.dram_tensor("ffnb_cols", [128, NK], F32, kind="ExternalInput").ap()
    alpha_bc = nc.dram_tensor("alpha_bc", [128, 1], F32, kind="ExternalInput").ap()
    ident_in = nc.dram_tensor("ident_in", [128, 128], F32, kind="ExternalInput").ap()
    ones_in = nc.dram_tensor("ones_in", [1, 128], F32, kind="ExternalInput").ap()
    onescol_in = nc.dram_tensor("onescol_in", [128, 1], F32, kind="ExternalInput").ap()
    out = nc.dram_tensor("out", [B, T, D], F32, kind="ExternalOutput").ap()
    win_h = nc.dram_tensor("win_h", [D, 2 * DI], BF16).ap()
    win_l = nc.dram_tensor("win_l", [D, 2 * DI], BF16).ap()
    wdt_h = nc.dram_tensor("wdt_h", [DI, DI], BF16).ap()
    wdt_l = nc.dram_tensor("wdt_l", [DI, DI], BF16).ap()
    wout_h = nc.dram_tensor("wout_h", [DI, D], BF16).ap()
    wout_l = nc.dram_tensor("wout_l", [DI, D], BF16).ap()
    wtT_scr = nc.dram_tensor("wtT_scr", [D, D], BF16).ap()

    S = lambda name, shape, dt=F32: nc.alloc_sbuf_tensor(name, shape, dt)

    x_tm = [S(f"x_tm{i}", [128, D]) for i in range(NTT)]
    ln_sc = S("ln_sc", [128, 16])
    bnst = S("bnst", [128, 12])
    XNW = 2 + CH
    xn_hl = [S(f"xnhl{j}", [128, 2 * XNW], BF16) for j in range(NK)]
    xnh = [t[:, 0:XNW] for t in xn_hl]
    xnl = [t[:, XNW:2 * XNW] for t in xn_hl]
    xi_hl = [S(f"xihl{j}", [128, 2 * CH], BF16) for j in range(NK)]
    xih = [t[:, 0:CH] for t in xi_hl]
    xil = [t[:, CH:2 * CH] for t in xi_hl]
    x_pre = [S(f"xpre{j}", [128, CH + 5]) for j in range(NJ)]
    conv_acc = [S(f"cacc{i}", [128, CS_MAX]) for i in range(2)]
    xf_hl = [S(f"xfhl{j}", [128, 2 * CS_MAX], BF16) for j in range(NJ)]
    xfh = [t[:, 0:CS_MAX] for t in xf_hl]
    xfl = [t[:, CS_MAX:2 * CS_MAX] for t in xf_hl]
    sz_buf = [S(f"sz{j}", [128, 2 + CH]) for j in range(NJ)]
    delta = [S(f"dl{j}", [128, CS_MAX]) for j in range(NJ)]
    wv = [S(f"wvv{j}", [128, CS_MAX]) for j in range(NJ)]
    y_t = [S(f"yy{j}", [128, CS_MAX]) for j in range(NJ)]
    y_bf = [S(f"ybf{j}", [128, 2 * CS_MAX], BF16) for j in range(NJ)]
    ybh = [t[:, 0:CS_MAX] for t in y_bf]
    ybl = [t[:, CS_MAX:2 * CS_MAX] for t in y_bf]
    NRING = 4
    h_rg = S("h_rg", [128, NRING * CS_MAX])
    a_t = [S(f"aa{i}", [128, CS_MAX]) for i in range(NRING)]
    b_t = [S(f"bb{i}", [128, CS_MAX]) for i in range(NRING)]
    cm_t = [S(f"cmm{i}", [128, CS_MAX]) for i in range(2)]
    NBREP = 3
    bcrep = [S(f"bcr{i}", [128, 2 * CS_MAX]) for i in range(NBREP)]
    brow = [S(f"brw{i}", [1, 2 * CS_MAX]) for i in range(2)]

    def hslot(j):
        return h_rg.ap()[:, (j % NRING) * CS_MAX:(j % NRING) * CS_MAX + CS_MAX]
    bc_sb = S("bc_sb", [32, CS_MAX])
    carries = S("carries", [128, NJ * DS])
    tails = S("tails", [128, 3 * NJ])
    s_sb = [[S(f"s_sb{p}_{o}", [128, CS_MAX]) for o in range(NK)] for p in range(2)]
    g_sb = [S(f"g_sb{o}", [128, CS_MAX]) for o in range(NK)]
    ostg = S("ostg0", [128, D])
    wb_h = [S(f"wbh{i}", [128, 512], BF16) for i in range(2)]
    wb_l = [S(f"wbl{i}", [128, 512], BF16) for i in range(2)]
    wx_h = S("wx_h", [128, NJ * 2 * DS], BF16)
    wx_l = S("wx_l", [128, NJ * 2 * DS], BF16)
    acol_sb = S("acol_sb", [128, NJ * DS])
    dcol_sb = S("dcol_sb", [128, NJ])
    cw_sb = S("cw_sb", [128, NJ * 4])
    cb_sb = S("cb_sb", [128, NJ])
    bdt_sb = S("bdt_sb", [128, NJ])
    ffnb_sb = S("ffnb_sb", [128, NK])
    alpha_sb = S("alpha_sb", [128, 1])
    ident_sb = S("ident_sb", [128, 128])
    ones_sb = S("ones_sb", [1, 128])
    onescol_sb = S("onescol_sb", [128, 1])
    thr_sb = S("thr_sb", [128, 2])
    eps_sb = S("eps_sb", [128, 1])
    fscr = x_tm[0]
    ftrn = x_tm[1]

    PS8 = [nc.alloc_psum_tensor(f"ps{i}", [128, 512], F32) for i in range(8)]
    P = PS8

    def pacc(i):
        return PS8[i].ap()

    t_x_tm = [TT(f"x_tm{i}") for i in range(NTT)]
    t_lnsc = TT("ln_sc"); t_bnst = TT("bnst")
    t_xnh = [TT(f"xnh{j}") for j in range(NK)]
    t_xnl = [TT(f"xnl{j}") for j in range(NK)]
    t_xih = [TT(f"xih{j}") for j in range(NK)]
    t_xil = [TT(f"xil{j}") for j in range(NK)]
    t_xpre = [TT(f"xpre{j}") for j in range(NJ)]
    t_cacc = [TT(f"cacc{i}") for i in range(2)]
    t_xfh = [TT(f"xfh{j}") for j in range(NJ)]
    t_xfl = [TT(f"xfl{j}") for j in range(NJ)]
    t_sz = [TT(f"sz{j}") for j in range(NJ)]
    t_dl = [TT(f"dl{j}") for j in range(NJ)]
    t_wv = [TT(f"wv{j}") for j in range(NJ)]
    t_y = [TT(f"y{j}") for j in range(NJ)]
    t_ybh = [TT(f"ybh{j}") for j in range(NJ)]
    t_ybl = [TT(f"ybl{j}") for j in range(NJ)]
    t_h = [TT(f"h{i}") for i in range(NRING)]
    t_a = [TT(f"a{i}") for i in range(NRING)]
    t_b = [TT(f"b{i}") for i in range(NRING)]
    t_cm = [TT(f"cm{i}") for i in range(2)]
    t_bcr = [TT(f"bcr{i}") for i in range(NBREP)]
    t_brow = [TT(f"brow{i}") for i in range(2)]
    t_bc = TT("bc")
    t_car = [[TT(f"car{s}_{g}") for g in range(NJ // 4)] for s in range(DS)]
    t_tails = TT("tails")
    t_s = [[TT(f"s{p}_{o}") for o in range(NK)] for p in range(2)]
    t_g = [TT(f"g{o}") for o in range(NK)]
    t_ostg = TT("ostg0")
    t_wbh = [TT(f"wbh{i}") for i in range(2)]
    t_wbl = [TT(f"wbl{i}") for i in range(2)]
    t_wx = TT("wx"); t_consts = TT("consts")
    t_thr = TT("thr"); t_fscr = t_x_tm[0]; t_ftrn = t_x_tm[1]
    t_P = [TT(f"P{i}") for i in range(8)]
    t_pacc = t_P
    t_out_hbm = TT("out_hbm")
    t_w_win = TT("w_win"); t_w_wdt = TT("w_wdt")
    t_w_wout = TT("w_wout"); t_w_ffn = TT("w_ffn")
    t_chain_h = TT("chain_h"); t_chain_l = TT("chain_l")
    t_fs4 = [TT(f"fs4_{i}") for i in range(4)]

    sem_const = kb.dma_sem("d_const")
    sem_xtm = [kb.dma_sem(f"d_xtm{i}") for i in range(NTT)]
    sem_wh = [kb.dma_sem(f"d_wh{i}") for i in range(2)]
    sem_wl = [kb.dma_sem(f"d_wl{i}") for i in range(2)]
    sem_row = [kb.dma_sem(f"d_row{i}") for i in range(2)]
    sem_oacc = kb.dma_sem("d_oacc0")
    sem_bulk = kb.dma_sem("d_bulk")
    sem_fprep = kb.dma_sem("d_fprep")
    sem_fprep4 = [kb.dma_sem(f"d_fp4_{i}") for i in range(4)]
    sem_fph = kb.dma_sem("d_fph")
    sem_fpl = kb.dma_sem("d_fpl")
    sem_wx = kb.dma_sem("d_wx")

    pe, dve, act, pool, sp = kb.pe, kb.dve, kb.act, kb.pool, kb.sp

    kb.dma(sp, sem_bulk, lambda e: e.dma_start(out=out, in_=inp),
           reads=(), writes=(t_out_hbm,))

    def cdma(fn):
        kb.dma(sp, sem_const, fn, reads=(), writes=(t_consts,))

    cdma(lambda e: e.dma_start(out=acol_sb[:], in_=a_cols))
    cdma(lambda e: e.dma_start(out=dcol_sb[:], in_=d_cols))
    cdma(lambda e: e.dma_start(out=cw_sb[:], in_=convw_cols))
    cdma(lambda e: e.dma_start(out=cb_sb[:], in_=convb_cols))
    cdma(lambda e: e.dma_start(out=bdt_sb[:], in_=bdt_cols))
    cdma(lambda e: e.dma_start(out=ffnb_sb[:], in_=ffnb_cols))
    cdma(lambda e: e.dma_start(out=alpha_sb[:], in_=alpha_bc))
    cdma(lambda e: e.dma_start(out=ident_sb[:], in_=ident_in))
    cdma(lambda e: e.dma_start(out=ones_sb[:], in_=ones_in))
    cdma(lambda e: e.dma_start(out=onescol_sb[:], in_=onescol_in))

    kb.emit(act, lambda e: e.activation(eps_sb[:], onescol_sb[:], AF.Copy,
                                        bias=0.0, scale=1e-5),
            reads=(t_consts,), writes=(t_consts,))

    for k in range(NJ):
        kb.dma(sp, sem_wx,
               lambda e, k=k: e.dma_start(out=fscr[:, 0:2 * DS],
                                          in_=w_x[k * 128:(k + 1) * 128, :]),
               reads=(), writes=(t_fscr,))
        kb.emit(act, lambda e, k=k: e.copy(wx_h[:, k * 2 * DS:(k + 1) * 2 * DS],
                                           fscr[:, 0:2 * DS]),
                reads=(t_fscr,), writes=(t_wx,))
        kb.emit(dve, lambda e, k=k: e.tensor_sub(wx_l[:, k * 2 * DS:(k + 1) * 2 * DS],
                                                 fscr[:, 0:2 * DS],
                                                 wx_h[:, k * 2 * DS:(k + 1) * 2 * DS]),
                reads=(t_fscr, t_wx), writes=(t_wx,))

    pctr = [0]

    def prep_weight(src_ap, dst_h, dst_l, rows, cols, tok):
        NR, NCP = rows // 128, cols // 512
        for r in range(NR):
            for cp in range(NCP):
                first = r == 0 and cp == 0
                last = r == NR - 1 and cp == NCP - 1
                c0 = cp * 512
                p = pctr[0]
                pctr[0] += 1
                fi = p % 4
                f, fc, tf = x_tm[fi // 2], (fi % 2) * 512, t_fs4[fi]
                lwr = (tf, t_x_tm[fi // 2]) if p < 4 else (tf,)
                yi = (2 * p) % 16
                hi_ap = y_bf[yi].ap()[:, 0:512]
                lo_ap = y_bf[yi + 1].ap()[:, 0:512]
                th = (t_ybh[yi], t_ybl[yi])
                tl = (t_ybh[yi + 1], t_ybl[yi + 1])
                kb.dma(sp, sem_fprep4[fi],
                       lambda e, r=r, c0=c0, src_ap=src_ap, f=f, fc=fc: e.dma_start(
                           out=f[:, fc:fc + 512],
                           in_=src_ap[r * 128:(r + 1) * 128, c0:c0 + 512]),
                       reads=(), writes=lwr)
                kb.emit(act, lambda e, f=f, fc=fc, hi_ap=hi_ap: e.copy(
                    hi_ap, f[:, fc:fc + 512]),
                    reads=(tf,), writes=th)
                kb.emit(dve, lambda e, f=f, fc=fc, hi_ap=hi_ap, lo_ap=lo_ap:
                        e.tensor_sub(lo_ap, f[:, fc:fc + 512], hi_ap),
                        reads=(tf,) + th, writes=tl)
                kb.dma(pool, sem_fph,
                       lambda e, r=r, c0=c0, dst_h=dst_h, hi_ap=hi_ap: e.dma_start(
                           out=dst_h[r * 128:(r + 1) * 128, c0:c0 + 512], in_=hi_ap),
                       reads=th + ((t_chain_h,) if first else ()),
                       writes=(tok,) + ((t_chain_h,) if last else ()))
                kb.dma(pool, sem_fpl,
                       lambda e, r=r, c0=c0, dst_l=dst_l, lo_ap=lo_ap: e.dma_start(
                           out=dst_l[r * 128:(r + 1) * 128, c0:c0 + 512], in_=lo_ap),
                       reads=tl + ((t_chain_l,) if first else ()),
                       writes=(tok,) + ((t_chain_l,) if last else ()))

    prep_weight(w_in, win_h, win_l, D, 2 * DI, t_w_win)
    prep_weight(w_dt, wdt_h, wdt_l, DI, DI, t_w_wdt)

    prep_weight(w_out, wout_h, wout_l, DI, D, t_w_wout)

    for k in range(NK):
        kb.dma(sp, sem_fprep,
               lambda e, k=k: e.dma_start(out=fscr[:], in_=ffn_w[k * 128:(k + 1) * 128, :]),
               reads=(), writes=(t_fscr, t_fs4[0], t_fs4[1]))
        kb.emit(dve, lambda e, k=k: e.tensor_reduce(ln_sc.ap()[:, k:k + 1], fscr.ap(),
                                                    AX.X, ALU.add, apply_absolute_value=True),
                reads=(t_fscr,), writes=(t_lnsc,))
    kb.emit(pe, lambda e: e.matmul(P[3].ap()[0:1, 0:8], onescol_sb[:], ln_sc.ap()[:, 0:8],
                                   start=True, stop=True),
            reads=(t_lnsc, t_consts), writes=(t_P[3],))
    kb.emit(dve, lambda e: e.tensor_reduce(ln_sc.ap()[0:1, 8:9], P[3].ap()[0:1, 0:8],
                                           AX.X, ALU.add),
            reads=(t_P[3],), writes=(t_lnsc,))
    kb.emit(pe, lambda e: e.matmul(P[3].ap()[:, 0:1], ones_sb[:], ln_sc.ap()[0:1, 8:9],
                                   start=True, stop=True),
            reads=(t_lnsc, t_consts), writes=(t_P[3],))
    kb.emit(act, lambda e: e.activation(thr_sb[:, 0:1], P[3].ap()[:, 0:1], AF.Copy,
                                        bias=0.0, scale=0.7 / (D * D)),
            reads=(t_P[3],), writes=(t_thr,))
    kb.emit(act, lambda e: e.activation(thr_sb[:, 1:2], thr_sb[:, 0:1], AF.Copy,
                                        bias=0.0, scale=-1.0),
            reads=(t_thr,), writes=(t_thr,))
    for r in range(NK):
        kb.dma(sp, sem_fprep,
               lambda e, r=r: e.dma_start(out=fscr[:], in_=ffn_w[r * 128:(r + 1) * 128, :]),
               reads=(), writes=(t_fscr, t_fs4[0], t_fs4[1]))
        kb.emit(dve, lambda e: e.tensor_scalar(ftrn.ap(), fscr.ap(), thr_sb[:, 0:1], None, ALU.is_gt),
                reads=(t_fscr, t_thr), writes=(t_ftrn, t_fs4[2], t_fs4[3]))
        kb.emit(dve, lambda e: e.tensor_scalar(ostg.ap(), fscr.ap(), thr_sb[:, 1:2], None, ALU.is_lt),
                reads=(t_fscr, t_thr), writes=(t_ostg,))
        kb.emit(dve, lambda e: e.tensor_sub(ftrn.ap(), ftrn.ap(), ostg.ap()),
                reads=(t_ostg, t_ftrn), writes=(t_ftrn,))
        grp = []
        for j in range(NK):
            bank = PS8[6] if j < 4 else PS8[7]
            def fn(e, j=j, bank=bank, first=(j % 4 == 0), last=(j == NK - 1)):
                return e.matmul(bank.ap()[:, (j % 4) * 128:(j % 4 + 1) * 128],
                                ftrn.ap()[:, j * 128:(j + 1) * 128], ident_sb[:],
                                start=first, stop=last, is_transpose=True,
                                skip_group_check=True)
            grp.append((fn, [t_ftrn, t_consts]))
        kb.mm_group(grp, (t_P[6], t_P[7]))
        kb.emit(act, lambda e: e.copy(wb_h[0][:], PS8[6].ap()),
                reads=(t_P[6],), writes=(t_wbh[0],))
        for j in range(4):
            kb.dma(pool, sem_fph,
                   lambda e, j=j, r=r: e.dma_start(
                       out=wtT_scr[j * 128:(j + 1) * 128, r * 128:(r + 1) * 128],
                       in_=wb_h[0][:, j * 128:(j + 1) * 128]),
                   reads=(t_wbh[0],) + ((t_chain_h,) if r == 0 and j == 0 else ()),
                   writes=(t_w_ffn,))
        kb.emit(act, lambda e: e.copy(wb_h[1][:], PS8[7].ap()),
                reads=(t_P[7],), writes=(t_wbh[1],))
        for j in range(4, 8):
            kb.dma(pool, sem_fph,
                   lambda e, j=j, r=r: e.dma_start(
                       out=wtT_scr[j * 128:(j + 1) * 128, r * 128:(r + 1) * 128],
                       in_=wb_h[1][:, (j - 4) * 128:(j - 3) * 128]),
                   reads=(t_wbh[1],), writes=(t_w_ffn,))



    def split_cols(src_fn, hi_ap, lo_ap, reads, whi, wlo):
        kb.emit(dve, lambda e: e.tensor_copy(hi_ap, src_fn()), reads=reads, writes=(whi,))
        kb.emit(dve, lambda e: e.tensor_sub(lo_ap, src_fn(), hi_ap),
                reads=tuple(reads) + (whi,), writes=(wlo,))

    def hl_rhs(hl_handle, W, off, colsS):
        return hl_handle.ap().rearrange("p (s w) -> p s w", w=W)[:, :, off:off + colsS]

    def mm_hl3(ma, gi, whi_ap, wlo_ap, hl_handle, W, off, colsS,
               reads_hi, reads_lo, t_x, first, last, rows=128):
        if 2 * colsS <= 512:
            rhs = hl_rhs(hl_handle, W, off, colsS)
            def f1(e, whi_ap=whi_ap, rhs=rhs, gi=gi, colsS=colsS, first=first, rows=rows):
                return e.matmul(pacc(gi)[0:rows, 0:2 * colsS], whi_ap, rhs,
                                start=first, stop=False, skip_group_check=True)
            ma.mm(f1, list(reads_hi) + list(t_x), gi)
            hi_s = hl_handle.ap()[:, off:off + colsS]
            def f2(e, wlo_ap=wlo_ap, hi_s=hi_s, gi=gi, colsS=colsS, last=last, rows=rows):
                return e.matmul(pacc(gi)[0:rows, 0:colsS], wlo_ap, hi_s,
                                start=False, stop=last, skip_group_check=True)
            ma.mm(f2, list(reads_lo) + list(t_x), gi)
        else:
            hi_s = hl_handle.ap()[:, off:off + colsS]
            lo_s = hl_handle.ap()[:, W + off:W + off + colsS]
            def f1(e, whi_ap=whi_ap, hi_s=hi_s, gi=gi, colsS=colsS, first=first, rows=rows):
                return e.matmul(pacc(gi)[0:rows, 0:colsS], whi_ap, hi_s,
                                start=first, stop=False, skip_group_check=True)
            ma.mm(f1, list(reads_hi) + list(t_x), gi)
            def f2(e, whi_ap=whi_ap, lo_s=lo_s, gi=gi, colsS=colsS, rows=rows):
                return e.matmul(pacc(gi)[0:rows, 0:colsS], whi_ap, lo_s,
                                start=False, stop=False, skip_group_check=True)
            ma.mm(f2, list(reads_hi) + list(t_x), gi)
            def f3(e, wlo_ap=wlo_ap, hi_s=hi_s, gi=gi, colsS=colsS, last=last, rows=rows):
                return e.matmul(pacc(gi)[0:rows, 0:colsS], wlo_ap, hi_s,
                                start=False, stop=last, skip_group_check=True)
            ma.mm(f3, list(reads_lo) + list(t_x), gi)

    def mm_hl1(ma, gi, w_ap, hl_handle, W, off, colsS, reads_w, t_x, first, last):
        if 2 * colsS <= 512:
            rhs = hl_rhs(hl_handle, W, off, colsS)
            def f1(e, w_ap=w_ap, rhs=rhs, gi=gi, colsS=colsS, first=first, last=last):
                return e.matmul(pacc(gi)[:, 0:2 * colsS], w_ap, rhs,
                                start=first, stop=last, skip_group_check=True)
            ma.mm(f1, list(reads_w) + list(t_x), gi)
        else:
            hi_s = hl_handle.ap()[:, off:off + colsS]
            lo_s = hl_handle.ap()[:, W + off:W + off + colsS]
            def f1(e, w_ap=w_ap, hi_s=hi_s, gi=gi, colsS=colsS, first=first):
                return e.matmul(pacc(gi)[:, 0:colsS], w_ap, hi_s,
                                start=first, stop=False, skip_group_check=True)
            ma.mm(f1, list(reads_w) + list(t_x), gi)
            def f2(e, w_ap=w_ap, lo_s=lo_s, gi=gi, colsS=colsS, last=last):
                return e.matmul(pacc(gi)[:, 0:colsS], w_ap, lo_s,
                                start=False, stop=last, skip_group_check=True)
            ma.mm(f2, list(reads_w) + list(t_x), gi)

    def psum2_ap(gi, colsS, rows=128):
        return PS8[gi].ap()[0:rows, 0:2 * colsS].rearrange("p (s w) -> p w s", s=2)

    def evac_sum2(eng, dst_fn, gi, colsS, extra_reads=(), writes=()):
        if 2 * colsS <= 512:
            kb.emit(eng, lambda e: e.tensor_reduce(dst_fn(), psum2_ap(gi, colsS),
                                                   AX.X, ALU.add),
                    reads=(t_pacc[gi],) + tuple(extra_reads), writes=writes)
        else:
            kb.emit(eng, lambda e: e.tensor_copy(dst_fn(), pacc(gi)[:, 0:colsS]),
                    reads=(t_pacc[gi],) + tuple(extra_reads), writes=writes)

    def evac_add2(o, par, gi, colsS):
        halves = [0, colsS] if 2 * colsS <= 512 else [0]
        for a in halves:
            kb.emit(dve, lambda e, o=o, par=par, gi=gi, a=a, colsS=colsS:
                    e.scalar_tensor_tensor(
                        s_sb[par][o][:, 0:colsS], pacc(gi)[:, a:a + colsS],
                        onescol_sb[:, 0:1], s_sb[par][o][:, 0:colsS],
                        ALU.mult, ALU.add),
                    reads=(t_pacc[gi], t_s[par][o], t_consts), writes=(t_s[par][o],))


    def params(n):
        b, c = divmod(n, NC_STEPS)
        s0 = max(0, c * CH - 2)
        e0 = T if c == NC_STEPS - 1 else (c + 1) * CH - 2
        return b, c, s0, e0 - s0, s0 - c * CH + 2, (c == NC_STEPS - 1)

    def pieces_input_ln(n):
        b, c = divmod(n, NC_STEPS)
        ps = []

        def p_dma(b=b, c=c):
            for i in range(NTT):
                kb.dma(sp, sem_xtm[i],
                       lambda e, i=i, b=b, c=c: e.dma_start(
                           out=x_tm[i][:],
                           in_=inp[b, c * CH + i * 128: c * CH + (i + 1) * 128, :]),
                       reads=(), writes=(t_x_tm[i],))
        ps.append(p_dma)

        for i in range(NTT):
            def p_ln(i=i):
                cb8 = 8 * i
                kb.emit(dve, lambda e, i=i: e.bn_stats(bnst.ap()[:, 0:6], x_tm[i][:, 0:512]),
                        reads=(t_x_tm[i],), writes=(t_bnst,))
                kb.emit(dve, lambda e, i=i: e.bn_stats(bnst.ap()[:, 6:12], x_tm[i][:, 512:1024]),
                        reads=(t_x_tm[i],), writes=(t_bnst,))
                kb.emit(dve, lambda e, cb8=cb8: e.bn_aggr(ln_sc.ap()[:, cb8:cb8 + 2], bnst.ap()),
                        reads=(t_bnst,), writes=(t_lnsc,))
                kb.emit(act, lambda e, cb8=cb8: e.activation(ln_sc.ap()[:, cb8 + 2:cb8 + 3],
                                                             ln_sc.ap()[:, cb8:cb8 + 1],
                                                             AF.Copy, bias=0.0, scale=-1.0),
                        reads=(t_lnsc,), writes=(t_lnsc,))
                kb.emit(act, lambda e, cb8=cb8: e.activation(ln_sc.ap()[:, cb8 + 6:cb8 + 7],
                                                             ln_sc.ap()[:, cb8 + 1:cb8 + 2],
                                                             AF.Sqrt, bias=eps_sb[:], scale=1.0),
                        reads=(t_lnsc, t_consts), writes=(t_lnsc,))
                kb.emit(dve, lambda e, cb8=cb8: e.reciprocal(ln_sc.ap()[:, cb8 + 3:cb8 + 4],
                                                             ln_sc.ap()[:, cb8 + 6:cb8 + 7]),
                        reads=(t_lnsc,), writes=(t_lnsc,))
                kb.emit(dve, lambda e, i=i, cb8=cb8: e.tensor_scalar(
                    x_tm[i][:], x_tm[i][:],
                    ln_sc.ap()[:, cb8 + 2:cb8 + 3], ln_sc.ap()[:, cb8 + 3:cb8 + 4],
                    ALU.add, op1=ALU.mult),
                    reads=(t_lnsc, t_x_tm[i]), writes=(t_x_tm[i],))
                kb.emit(dve, lambda e, i=i, cb8=cb8: e.tensor_reduce(
                    ln_sc.ap()[:, cb8 + 4:cb8 + 5], x_tm[i][:], AX.X, ALU.add),
                    reads=(t_x_tm[i],), writes=(t_lnsc,))
                kb.emit(act, lambda e, cb8=cb8: e.activation(
                    ln_sc.ap()[:, cb8 + 5:cb8 + 6], ln_sc.ap()[:, cb8 + 4:cb8 + 5],
                    AF.Copy, bias=0.0, scale=-1.0 / D),
                    reads=(t_lnsc,), writes=(t_lnsc,))
            ps.append(p_ln)
        return ps

    def pieces_transposes(n):
        b, c, s0, colsS, base2, last = params(n)
        ps = []
        for j in range(NK):
            def p(j=j, c=c):
                if c > 0:
                    kb.emit(act, lambda e, j=j: e.copy(xnh[j][:, 0:2], xnh[j][:, CH:CH + 2]),
                            reads=(t_xnh[j],), writes=(t_xnh[j],))
                    kb.emit(pool, lambda e, j=j: e.tensor_copy(xnl[j][:, 0:2], xnl[j][:, CH:CH + 2]),
                            reads=(t_xnl[j],), writes=(t_xnl[j],))
                grp = []
                for i in range(NTT):
                    def fn(e, j=j, i=i, first=(i == 0), last_=(i == NTT - 1)):
                        return e.matmul(P[5].ap()[:, i * 128:(i + 1) * 128],
                                        x_tm[i][:, j * 128:(j + 1) * 128], ident_sb[:],
                                        start=first, stop=last_, is_transpose=True,
                                        skip_group_check=True)
                    grp.append((fn, [t_x_tm[i], t_consts]))
                kb.mm_group(grp, (t_P[5],))
                split_cols(lambda: P[5].ap()[:, 0:CH],
                           xnh[j][:, 2:2 + CH], xnl[j][:, 2:2 + CH],
                           (t_P[5],), t_xnh[j], t_xnl[j])
            ps.append(p)

        def p_msub():
            for i in range(NTT):
                cb8 = 8 * i
                kb.emit(dve, lambda e, i=i, cb8=cb8: e.tensor_scalar(
                    x_tm[i][:], x_tm[i][:], ln_sc.ap()[:, cb8 + 5:cb8 + 6], None, ALU.add),
                    reads=(t_lnsc, t_x_tm[i]), writes=(t_x_tm[i],))
        ps.append(p_msub)

        for j in range(NK):
            def p(j=j):
                grp = []
                for i in range(NTT):
                    def fn(e, j=j, i=i, first=(i == 0), last_=(i == NTT - 1)):
                        return e.matmul(P[5].ap()[:, i * 128:(i + 1) * 128],
                                        x_tm[i][:, j * 128:(j + 1) * 128], ident_sb[:],
                                        start=first, stop=last_, is_transpose=True,
                                        skip_group_check=True)
                    grp.append((fn, [t_x_tm[i], t_consts]))
                kb.mm_group(grp, (t_P[5],))
                split_cols(lambda: P[5].ap()[:, 0:CH],
                           xih[j][:], xil[j][:],
                           (t_P[5],), t_xih[j], t_xil[j])
            ps.append(p)
        return ps

    def pieces_xz_x(n):
        ps = []
        for blk in range(4):
            F0 = blk * 4
            st = {}

            def p_k(k, F0=F0, st=st):
                if 'ma' not in st:
                    st['ma'] = MultiAcc(kb, t_P)
                ma = st['ma']
                slot = k % 2
                kb.dma(sp, sem_wh[slot],
                       lambda e, k=k, F0=F0, slot=slot: e.dma_start(
                           out=wb_h[slot][:],
                           in_=win_h[k * 128:(k + 1) * 128, F0 * 128:F0 * 128 + 512]),
                       reads=(t_w_win,), writes=(t_wbh[slot],))
                kb.dma(sp, sem_wl[slot],
                       lambda e, k=k, F0=F0, slot=slot: e.dma_start(
                           out=wb_l[slot][:],
                           in_=win_l[k * 128:(k + 1) * 128, F0 * 128:F0 * 128 + 512]),
                       reads=(t_w_win,), writes=(t_wbl[slot],))
                for fi in range(4):
                    mm_hl3(ma, fi,
                           wb_h[slot][:, fi * 128:(fi + 1) * 128],
                           wb_l[slot][:, fi * 128:(fi + 1) * 128],
                           xi_hl[k], CH, 0, CH,
                           (t_wbh[slot],), (t_wbl[slot],), (t_xih[k], t_xil[k]),
                           first=(k == 0), last=(k == NK - 1))
            for k in range(NK):
                ps.append(lambda k=k, p_k=p_k: p_k(k))

            def p_evac(F0=F0, st=st):
                st['ma'].finish()
                for fi in range(4):
                    jj = F0 + fi
                    evac_sum2(dve, lambda jj=jj: x_pre[jj][:, 3:3 + CH], fi, CH,
                              writes=(t_xpre[jj],))
            ps.append(p_evac)
        return ps

    def pieces_z(n):
        b, c, s0, colsS, base2, last = params(n)
        ps = []
        for blk in range(4):
            F0 = NJ + blk * 4
            st = {}

            def p_k(k, F0=F0, st=st):
                if 'ma' not in st:
                    st['ma'] = MultiAcc(kb, t_P)
                ma = st['ma']
                slot = k % 2
                kb.dma(sp, sem_wh[slot],
                       lambda e, k=k, F0=F0, slot=slot: e.dma_start(
                           out=wb_h[slot][:],
                           in_=win_h[k * 128:(k + 1) * 128, F0 * 128:F0 * 128 + 512]),
                       reads=(t_w_win,), writes=(t_wbh[slot],))
                kb.dma(sp, sem_wl[slot],
                       lambda e, k=k, F0=F0, slot=slot: e.dma_start(
                           out=wb_l[slot][:],
                           in_=win_l[k * 128:(k + 1) * 128, F0 * 128:F0 * 128 + 512]),
                       reads=(t_w_win,), writes=(t_wbl[slot],))
                for fi in range(4):
                    mm_hl3(ma, fi,
                           wb_h[slot][:, fi * 128:(fi + 1) * 128],
                           wb_l[slot][:, fi * 128:(fi + 1) * 128],
                           xi_hl[k], CH, 0, CH,
                           (t_wbh[slot],), (t_wbl[slot],), (t_xih[k], t_xil[k]),
                           first=(k == 0), last=(k == NK - 1))
            for k in range(NK):
                ps.append(lambda k=k, p_k=p_k: p_k(k))

            def p_evac(F0=F0, st=st, c=c):
                st['ma'].finish()
                for fi in range(4):
                    jj = F0 - NJ + fi
                    if c > 0:
                        kb.emit(act, lambda e, jj=jj: e.copy(sz_buf[jj][:, 0:2],
                                                             sz_buf[jj][:, CH:CH + 2]),
                                reads=(t_sz[jj],), writes=(t_sz[jj],))
                    stg2, tstg2 = conv_acc[fi % 2], t_cacc[fi % 2]
                    evac_sum2(dve, lambda stg2=stg2: stg2.ap()[:, 0:CH], fi, CH,
                              writes=(tstg2,))
                    kb.emit(act, lambda e, jj=jj, stg2=stg2: e.activation(
                        sz_buf[jj][:, 2:2 + CH], stg2.ap()[:, 0:CH], AF.Silu,
                        bias=0.0, scale=1.0),
                        reads=(tstg2,), writes=(t_sz[jj],))
            ps.append(p_evac)
        return ps

    def pieces_conv(n):
        b, c, s0, colsS, base2, last = params(n)
        off0 = base2
        ps = []
        if c == 0:
            ps.append(lambda: kb.emit(dve, lambda e: e.memset(tails.ap(), 0.0),
                                      reads=(), writes=(t_tails,)))
        for jj in range(NJ):
            def p(jj=jj, off0=off0, colsS=colsS, last=last):
                kb.emit(act, lambda e, jj=jj: e.copy(x_pre[jj][:, 0:3],
                                                     tails.ap()[:, jj * 3:jj * 3 + 3]),
                        reads=(t_tails,), writes=(t_xpre[jj],))
                if last:
                    kb.emit(dve, lambda e, jj=jj: e.memset(x_pre[jj][:, 3 + CH:5 + CH], 0.0),
                            reads=(), writes=(t_xpre[jj],))
                acc, tacc = conv_acc[jj % 2], t_cacc[jj % 2]
                kb.emit(dve, lambda e, jj=jj, acc=acc, off0=off0, colsS=colsS:
                        e.tensor_scalar(
                            acc.ap()[:, 0:colsS],
                            x_pre[jj][:, off0: off0 + colsS],
                            cw_sb[:, jj * 4: jj * 4 + 1], None, ALU.mult),
                        reads=(t_xpre[jj], t_consts), writes=(tacc,))
                for k4 in range(1, 4):
                    kb.emit(dve, lambda e, jj=jj, acc=acc, k4=k4, off0=off0, colsS=colsS:
                            e.scalar_tensor_tensor(
                                acc.ap()[:, 0:colsS],
                                x_pre[jj][:, off0 + k4: off0 + k4 + colsS],
                                cw_sb[:, jj * 4 + k4: jj * 4 + k4 + 1],
                                acc.ap()[:, 0:colsS], ALU.mult, ALU.add),
                            reads=(t_xpre[jj], t_consts, tacc), writes=(tacc,))
                if not last:
                    kb.emit(act, lambda e, jj=jj: e.copy(tails.ap()[:, jj * 3:jj * 3 + 3],
                                                         x_pre[jj][:, CH:CH + 3]),
                            reads=(t_xpre[jj],), writes=(t_tails,))
                kb.emit(act, lambda e, jj=jj, acc=acc, colsS=colsS: e.activation(
                    x_pre[jj][:, 0:colsS], acc.ap()[:, 0:colsS], AF.Silu,
                    bias=cb_sb[:, jj:jj + 1], scale=1.0),
                    reads=(tacc, t_consts), writes=(t_xpre[jj],))
                split_cols(lambda jj=jj, colsS=colsS: x_pre[jj][:, 0:colsS],
                           xfh[jj][:, 0:colsS], xfl[jj][:, 0:colsS],
                           (t_xpre[jj],), t_xfh[jj], t_xfl[jj])
            ps.append(p)
        return ps

    def pieces_delta(n):
        b, c, s0, colsS, base2, last = params(n)
        blks = []
        for blk in range(4):
            j0 = blk * 4
            with_wx = blk == 0
            st = {}
            ps = []

            def p_k(k, j0=j0, st=st, colsS=colsS, with_wx=with_wx):
                if 'ma' not in st:
                    st['ma'] = MultiAcc(kb, t_P)
                ma = st['ma']
                slot = k % 2
                kb.dma(sp, sem_wh[slot],
                       lambda e, k=k, slot=slot, j0=j0: e.dma_start(
                           out=wb_h[slot][:],
                           in_=wdt_h[k * 128:(k + 1) * 128, j0 * 128:j0 * 128 + 512]),
                       reads=(t_w_wdt,), writes=(t_wbh[slot],))
                kb.dma(sp, sem_wl[slot],
                       lambda e, k=k, slot=slot, j0=j0: e.dma_start(
                           out=wb_l[slot][:],
                           in_=wdt_l[k * 128:(k + 1) * 128, j0 * 128:j0 * 128 + 512]),
                       reads=(t_w_wdt,), writes=(t_wbl[slot],))
                for ji in range(4):
                    mm_hl3(ma, ji,
                           wb_h[slot][:, ji * 128:(ji + 1) * 128],
                           wb_l[slot][:, ji * 128:(ji + 1) * 128],
                           xf_hl[k], CS_MAX, 0, colsS,
                           (t_wbh[slot],), (t_wbl[slot],), (t_xfh[k], t_xfl[k]),
                           first=(k == 0), last=(k == NJ - 1))
                if with_wx:
                    mm_hl3(ma, 4,
                           wx_h[:, k * 2 * DS:(k + 1) * 2 * DS],
                           wx_l[:, k * 2 * DS:(k + 1) * 2 * DS],
                           xf_hl[k], CS_MAX, 0, colsS,
                           (t_wx,), (t_wx,), (t_xfh[k], t_xfl[k]),
                           first=(k == 0), last=(k == NJ - 1), rows=32)
            for k in range(NJ):
                ps.append(lambda k=k, p_k=p_k: p_k(k))

            def p_evac(j0=j0, st=st, colsS=colsS, with_wx=with_wx):
                st['ma'].finish()
                for ji in range(4):
                    j = j0 + ji
                    evac_sum2(dve, lambda j=j, colsS=colsS: delta[j][:, 0:colsS],
                              ji, colsS, writes=(t_dl[j],))
                if with_wx:
                    if 2 * colsS <= 512:
                        kb.emit(dve, lambda e, colsS=colsS: e.tensor_reduce(
                            bc_sb[0:32, 0:colsS], psum2_ap(4, colsS, rows=32),
                            AX.X, ALU.add),
                            reads=(t_pacc[4],), writes=(t_bc,))
                    else:
                        kb.emit(dve, lambda e, colsS=colsS: e.tensor_copy(
                            bc_sb[0:32, 0:colsS], pacc(4)[0:32, 0:colsS]),
                            reads=(t_pacc[4],), writes=(t_bc,))
            ps.append(p_evac)
            blks.append(ps)
        return blks

    def em_post_q(n, qb):
        b, c, s0, colsS, base2, last = params(n)
        js = list(range(qb * 4, (qb + 1) * 4))
        for j in js:
            kb.emit(act, lambda e, j=j, colsS=colsS: e.activation(
                wv[j][:, 0:colsS], delta[j][:, 0:colsS], AF.Sigmoid,
                bias=bdt_sb[:, j:j + 1], scale=1.0),
                reads=(t_dl[j], t_consts), writes=(t_wv[j],))
        for j in js:
            kb.emit(act, lambda e, j=j, colsS=colsS: e.activation(
                wv[j][:, 0:colsS], wv[j][:, 0:colsS], AF.Ln, bias=0.0, scale=1.0),
                reads=(t_wv[j],), writes=(t_wv[j],))
        for j in js:
            kb.emit(dve, lambda e, j=j, colsS=colsS: e.scalar_tensor_tensor(
                delta[j][:, 0:colsS], delta[j][:, 0:colsS],
                bdt_sb[:, j:j + 1], wv[j][:, 0:colsS],
                ALU.add, ALU.subtract),
                reads=(t_wv[j], t_dl[j], t_consts), writes=(t_dl[j],))
        for j in js:
            kb.emit(pool, lambda e, j=j, colsS=colsS: e.tensor_add(
                y_t[j][:, 0:colsS], xfh[j][:, 0:colsS], xfl[j][:, 0:colsS]),
                reads=(t_xfh[j], t_xfl[j]), writes=(t_y[j],))
            kb.emit(pool, lambda e, j=j, colsS=colsS: e.tensor_mul(
                wv[j][:, 0:colsS], delta[j][:, 0:colsS], y_t[j][:, 0:colsS]),
                reads=(t_dl[j], t_y[j]), writes=(t_wv[j],))
            kb.emit(act, lambda e, j=j, colsS=colsS: e.activation(
                y_t[j][:, 0:colsS], y_t[j][:, 0:colsS], AF.Identity,
                bias=0.0, scale=dcol_sb[:, j:j + 1]),
                reads=(t_y[j], t_consts), writes=(t_y[j],))

    def em_prefill(n):
        b, c, s0, colsS, base2, last = params(n)
        par = n % 2
        for o in range(NK):
            kb.emit(pool, lambda e, o=o, par=par, base2=base2, colsS=colsS: e.tensor_add(
                s_sb[par][o][:, 0:colsS],
                xnh[o][:, base2:base2 + colsS], xnl[o][:, base2:base2 + colsS]),
                reads=(t_xnh[o], t_xnl[o]), writes=(t_s[par][o],))
        if c == 0:
            kb.emit(dve, lambda e: e.memset(carries.ap(), 0.0), reads=(),
                    writes=tuple(t for row in t_car for t in row))

    BCBANKS = (4, 6, 7)

    def em_bcast(n, jb, s):
        b, c, s0, colsS, base2, last = params(n)
        si = jb * DS + s
        rb = si % 2
        sb_i = si % NBREP
        bk = BCBANKS[si % 3]
        kb.dma(sp, sem_row[rb],
               lambda e, s=s, rb=rb, colsS=colsS: e.dma_start(
                   out=brow[rb][0:1, 0:colsS], in_=bc_sb[s:s + 1, 0:colsS]),
               reads=(t_bc,), writes=(t_brow[rb],))
        if 2 * colsS <= 512:
            kb.dma(sp, sem_row[rb],
                   lambda e, s=s, rb=rb, colsS=colsS: e.dma_start(
                       out=brow[rb][0:1, colsS:2 * colsS],
                       in_=bc_sb[DS + s:DS + s + 1, 0:colsS]),
                   reads=(t_bc,), writes=(t_brow[rb],))
            kb.emit(pe, lambda e, rb=rb, bk=bk, colsS=colsS: e.matmul(
                P[bk].ap()[:, 0:2 * colsS], ones_sb[:], brow[rb][0:1, 0:2 * colsS],
                start=True, stop=True),
                reads=(t_brow[rb], t_consts), writes=(t_P[bk],))
            kb.emit(act, lambda e, sb_i=sb_i, bk=bk, colsS=colsS: e.copy(
                bcrep[sb_i][:, 0:2 * colsS], P[bk].ap()[:, 0:2 * colsS]),
                reads=(t_P[bk],), writes=(t_bcr[sb_i],))
        else:
            kb.emit(pe, lambda e, rb=rb, bk=bk, colsS=colsS: e.matmul(
                P[bk].ap()[:, 0:colsS], ones_sb[:], brow[rb][0:1, 0:colsS],
                start=True, stop=True),
                reads=(t_brow[rb], t_consts), writes=(t_P[bk],))
            kb.emit(act, lambda e, sb_i=sb_i, bk=bk, colsS=colsS: e.copy(
                bcrep[sb_i][:, 0:colsS], P[bk].ap()[:, 0:colsS]),
                reads=(t_P[bk],), writes=(t_bcr[sb_i],))
            kb.dma(sp, sem_row[rb],
                   lambda e, s=s, rb=rb, colsS=colsS: e.dma_start(
                       out=brow[rb][0:1, CS_MAX:CS_MAX + colsS],
                       in_=bc_sb[DS + s:DS + s + 1, 0:colsS]),
                   reads=(t_bc,), writes=(t_brow[rb],))
            kb.emit(pe, lambda e, rb=rb, bk=bk, colsS=colsS: e.matmul(
                P[bk].ap()[:, 0:colsS], ones_sb[:],
                brow[rb][0:1, CS_MAX:CS_MAX + colsS],
                start=True, stop=True),
                reads=(t_brow[rb], t_consts, t_bcr[sb_i]), writes=(t_P[bk],))
            kb.emit(act, lambda e, sb_i=sb_i, bk=bk, colsS=colsS: e.copy(
                bcrep[sb_i][:, CS_MAX:CS_MAX + colsS], P[bk].ap()[:, 0:colsS]),
                reads=(t_P[bk],), writes=(t_bcr[sb_i],))

    def em_extract(n, jb, s):
        b, c, s0, colsS, base2, last = params(n)

        def carry_fn(e, s=s, jb=jb, colsS=colsS):
            hv = h_rg.ap().rearrange("p (j cc) -> p j cc", cc=CS_MAX)
            src = hv[:, 0:4, colsS - 1:colsS]
            cv = carries.ap().rearrange("p (j s2) -> p j s2", s2=DS)
            dst = cv[:, jb * 4:(jb + 1) * 4, s:s + 1]
            return e.tensor_copy(dst, src)
        kb.emit(pool, carry_fn, reads=tuple(t_h), writes=(t_car[s][jb],))

    def em_scan_hs(n, hb, s, pend):
        b, c, s0, colsS, base2, last = params(n)
        sb_i = (hb * DS + s) % NBREP
        coff = colsS if 2 * colsS <= 512 else CS_MAX
        for gi in range(2):
            g4 = 2 * hb + gi
            js = list(range(g4 * 4, (g4 + 1) * 4))
            for j in js:
                ai = j % NRING
                kb.emit(act, lambda e, j=j, s=s, ai=ai, colsS=colsS: e.activation(
                    a_t[ai][:, 0:colsS], delta[j][:, 0:colsS], AF.Exp,
                    bias=0.0, scale=acol_sb[:, j * DS + s: j * DS + s + 1]),
                    reads=(t_dl[j], t_consts), writes=(t_a[ai],))
            if pend[0] is not None:
                em_extract(n, pend[0][0], pend[0][1])
            pend[0] = (g4, s)
            for j in js:
                ai = j % NRING
                kb.emit(dve, lambda e, j=j, ai=ai, sb_i=sb_i, colsS=colsS: e.tensor_mul(
                    b_t[ai][:, 0:colsS], wv[j][:, 0:colsS], bcrep[sb_i][:, 0:colsS]),
                    reads=(t_wv[j], t_bcr[sb_i]), writes=(t_b[ai],))
            for j in js:
                ai = j % NRING
                cidx = j * DS + s
                kb.emit(dve, lambda e, j=j, ai=ai, cidx=cidx, colsS=colsS:
                        e.tensor_tensor_scan(
                            hslot(j)[:, 0:colsS],
                            a_t[ai][:, 0:colsS], b_t[ai][:, 0:colsS],
                            carries.ap()[:, cidx:cidx + 1], ALU.mult, ALU.add),
                        reads=(t_a[ai], t_b[ai], t_car[s][g4]), writes=(t_h[ai],))
            for j in js:
                ai = j % NRING
                ceng = dve if j % 2 == 0 else pool
                yeng = ceng
                ci = j % 2
                kb.emit(ceng, lambda e, j=j, ci=ci, sb_i=sb_i, coff=coff, colsS=colsS:
                        e.tensor_mul(
                            cm_t[ci][:, 0:colsS],
                            hslot(j)[:, 0:colsS],
                            bcrep[sb_i][:, coff:coff + colsS]),
                        reads=(t_h[ai], t_bcr[sb_i]), writes=(t_cm[ci],))
                kb.emit(yeng, lambda e, j=j, ci=ci, colsS=colsS: e.tensor_add(
                    y_t[j][:, 0:colsS], y_t[j][:, 0:colsS], cm_t[ci][:, 0:colsS]),
                    reads=(t_cm[ci], t_y[j]), writes=(t_y[j],))

    def em_yfin(n):
        b, c, s0, colsS, base2, last = params(n)
        for j in range(NJ):
            kb.emit(dve, lambda e, j=j, base2=base2, colsS=colsS: e.tensor_mul(
                y_t[j][:, 0:colsS], y_t[j][:, 0:colsS],
                sz_buf[j][:, base2:base2 + colsS]),
                reads=(t_sz[j], t_y[j]), writes=(t_y[j],))
            split_cols(lambda j=j, colsS=colsS: y_t[j][:, 0:colsS],
                       ybh[j][:, 0:colsS], ybl[j][:, 0:colsS],
                       (t_y[j],), t_ybh[j], t_ybl[j])

    def pieces_wout(n):
        b, c, s0, colsS, base2, last = params(n)
        par = n % 2
        ps = []
        for blk in range(2):
            o0 = blk * 4
            st = {}

            def p_k(k, o0=o0, st=st, colsS=colsS):
                if 'ma' not in st:
                    st['ma'] = MultiAcc(kb, t_P)
                ma = st['ma']
                slot = k % 2
                kb.dma(sp, sem_wh[slot],
                       lambda e, k=k, slot=slot, o0=o0: e.dma_start(
                           out=wb_h[slot][:],
                           in_=wout_h[k * 128:(k + 1) * 128, o0 * 128:o0 * 128 + 512]),
                       reads=(t_w_wout,), writes=(t_wbh[slot],))
                kb.dma(sp, sem_wl[slot],
                       lambda e, k=k, slot=slot, o0=o0: e.dma_start(
                           out=wb_l[slot][:],
                           in_=wout_l[k * 128:(k + 1) * 128, o0 * 128:o0 * 128 + 512]),
                       reads=(t_w_wout,), writes=(t_wbl[slot],))
                for oi in range(4):
                    mm_hl3(ma, oi,
                           wb_h[slot][:, oi * 128:(oi + 1) * 128],
                           wb_l[slot][:, oi * 128:(oi + 1) * 128],
                           y_bf[k], CS_MAX, 0, colsS,
                           (t_wbh[slot],), (t_wbl[slot],), (t_ybh[k], t_ybl[k]),
                           first=(k == 0), last=(k == NJ - 1))
            for k in range(NJ):
                ps.append(lambda k=k, p_k=p_k: p_k(k))

            def p_evac(o0=o0, st=st, colsS=colsS, par=par):
                st['ma'].finish()
                for oi in range(4):
                    evac_add2(o0 + oi, par, oi, colsS)
            ps.append(p_evac)
        return ps

    def pieces_prodout(n):
        b, c, s0, colsS, base2, last = params(n)
        par = n % 2
        ps = []

        def p_prod():
            for o in range(NK):
                kb.emit(pool, lambda e, o=o, par=par, colsS=colsS: e.tensor_mul(
                    s_sb[par][o][:, 0:colsS], s_sb[par][o][:, 0:colsS],
                    g_sb[o][:, 0:colsS]),
                    reads=(t_g[o], t_s[par][o]), writes=(t_s[par][o],))
        ps.append(p_prod)

        nsub = (colsS + 127) // 128
        for ti in range(nsub):
            def p_sub(ti=ti, s0=s0, colsS=colsS, par=par, b=b):
                u0 = ti * 128
                Fz = min(128, colsS - u0)
                for rnd in range(2):
                    grp = []
                    for oi in range(4):
                        o = rnd * 4 + oi
                        def fn(e, o=o, oi=oi, u0=u0, Fz=Fz, par=par,
                               first=(oi == 0), last_=(oi == 3)):
                            return e.matmul(P[5].ap()[0:Fz, oi * 128:(oi + 1) * 128],
                                            s_sb[par][o][:, u0:u0 + Fz], ident_sb[:],
                                            start=first, stop=last_, is_transpose=True,
                                            skip_group_check=True)
                        grp.append((fn, [t_s[par][o], t_consts]))
                    kb.mm_group(grp, (t_P[5],))
                    kb.emit(act, lambda e, rnd=rnd, Fz=Fz: e.copy(
                        ostg.ap()[0:Fz, rnd * 512:(rnd + 1) * 512], P[5].ap()[0:Fz, :]),
                        reads=(t_P[5],), writes=(t_ostg,))
                tok0 = s0 + u0
                kb.dma(pool, sem_oacc,
                       lambda e, Fz=Fz, tok0=tok0, b=b: e.dma_start(
                           out=out[b, tok0:tok0 + Fz, :], in_=ostg[0:Fz, :],
                           accum_op=ALU.add),
                       reads=(t_ostg,), writes=(t_out_hbm,))
            ps.append(p_sub)
        return ps

    def pieces_ffn(n):
        b, c, s0, colsS, base2, last = params(n)
        ps = []
        for blk in range(2):
            o0 = blk * 4
            st = {}

            def p_k(k, o0=o0, st=st, colsS=colsS, base2=base2):
                if 'ma' not in st:
                    st['ma'] = MultiAcc(kb, t_P)
                ma = st['ma']
                slot = k % 2
                kb.dma(sp, sem_wh[slot],
                       lambda e, k=k, slot=slot, o0=o0: e.dma_start(
                           out=wb_h[slot][:],
                           in_=wtT_scr[k * 128:(k + 1) * 128, o0 * 128:o0 * 128 + 512]),
                       reads=(t_w_ffn,), writes=(t_wbh[slot],))
                for oi in range(4):
                    mm_hl1(ma, oi,
                           wb_h[slot][:, oi * 128:(oi + 1) * 128],
                           xn_hl[k], XNW, base2, colsS,
                           (t_wbh[slot],), (t_xnh[k], t_xnl[k]),
                           first=(k == 0), last=(k == NK - 1))
            for k in range(NK):
                ps.append(lambda k=k, p_k=p_k: p_k(k))

            def p_evac(o0=o0, st=st, colsS=colsS):
                st['ma'].finish()
                for oi in range(4):
                    o = o0 + oi
                    if 2 * colsS <= 512:
                        stg2, tstg2 = conv_acc[oi % 2], t_cacc[oi % 2]
                        evac_sum2(dve, lambda stg2=stg2, colsS=colsS: stg2.ap()[:, 0:colsS],
                                  oi, colsS, writes=(tstg2,))
                        kb.emit(act, lambda e, o=o, stg2=stg2, colsS=colsS: e.activation(
                            g_sb[o][:, 0:colsS], stg2.ap()[:, 0:colsS], AF.Silu,
                            bias=ffnb_sb[:, o:o + 1], scale=alpha_sb[:, 0:1]),
                            reads=(tstg2, t_consts), writes=(t_g[o],))
                    else:
                        kb.emit(act, lambda e, o=o, oi=oi, colsS=colsS: e.activation(
                            g_sb[o][:, 0:colsS], pacc(oi)[:, 0:colsS], AF.Silu,
                            bias=ffnb_sb[:, o:o + 1], scale=alpha_sb[:, 0:1]),
                            reads=(t_pacc[oi], t_consts), writes=(t_g[o],))
            ps.append(p_evac)
        return ps

    NQ = 2 * DS

    def scan_window(n, ps):
        rem = list(ps)
        pend = [None]
        LOOK = 3
        for si in range(NQ):
            hb, s = divmod(si, DS)
            if si == 0:
                for la in range(LOOK):
                    em_bcast(n, la // DS, la % DS)
            em_scan_hs(n, hb, s, pend)
            if si + LOOK < NQ:
                em_bcast(n, (si + LOOK) // DS, (si + LOOK) % DS)
            if rem:
                q = -(-len(rem) // max(1, NQ - 2 - si))
                for _ in range(min(q, len(rem))):
                    rem.pop(0)()
        em_extract(n, pend[0][0], pend[0][1])
        while rem:
            rem.pop(0)()

    for p in (pieces_input_ln(0) + pieces_transposes(0) + pieces_xz_x(0)
              + pieces_conv(0)):
        p()

    for n in range(NSTEP):
        if n > 0:
            em_yfin(n - 1)
        dblks = pieces_delta(n)
        for p in dblks[0] + dblks[1]:
            p()
        em_post_q(n, 0)
        em_post_q(n, 1)
        em_prefill(n)
        ps = []
        for qb in (2, 3):
            ps += dblks[qb]
            ps.append(lambda n=n, qb=qb: em_post_q(n, qb))
        ps += pieces_z(n)
        if n > 0:
            ps += pieces_wout(n - 1) + pieces_prodout(n - 1)
        ps += pieces_ffn(n)
        if n + 1 < NSTEP:
            ps += (pieces_input_ln(n + 1) + pieces_transposes(n + 1)
                   + pieces_xz_x(n + 1) + pieces_conv(n + 1))
        scan_window(n, ps)

    em_yfin(NSTEP - 1)
    for p in pieces_wout(NSTEP - 1) + pieces_prodout(NSTEP - 1):
        p()

    def final(sync):
        sync.wait_ge(sem_bulk, kb.dma_counts[id(sem_bulk)])
        if kb.dma_counts[id(sem_oacc)]:
            sync.wait_ge(sem_oacc, kb.dma_counts[id(sem_oacc)])
    kb.sp.thunks.append(final)

    with nc.Block() as block:
        @block.sync
        def _(sync):
            for th in kb.sp.thunks:
                th(sync)

        @block.tensor
        def _(te):
            for th in kb.pe.thunks:
                th(te)

        @block.vector
        def _(v):
            for th in kb.dve.thunks:
                th(v)

        @block.scalar
        def _(sc):
            for th in kb.act.thunks:
                th(sc)

        @block.gpsimd
        def _(g):
            for th in kb.pool.thunks:
                th(g)

    return nc



def make_in_map(inp_shard, W):
    a_cols = np.ascontiguousarray(
        np.asarray(W["L0_attn_A"]).reshape(NJ, 128, DS).transpose(1, 0, 2).reshape(128, -1))
    d_cols = np.ascontiguousarray(np.asarray(W["L0_attn_D"]).reshape(NJ, 128).T)
    convw = np.asarray(W["L0_attn_conv_w"])[:, 0, :]
    convw_cols = np.ascontiguousarray(convw.reshape(4, NJ, 128).transpose(2, 1, 0).reshape(128, -1))
    convb_cols = np.ascontiguousarray(np.asarray(W["L0_attn_conv_b"]).reshape(NJ, 128).T)
    bdt_cols = np.ascontiguousarray(np.asarray(W["L0_attn_b_dt"]).reshape(NJ, 128).T)
    ffnb_cols = np.ascontiguousarray(np.asarray(W["L0_ffn_bias"]).reshape(NK, 128).T)
    alpha_bc = np.ascontiguousarray(
        np.broadcast_to(np.asarray(W["L0_ffn_alpha"]).reshape(1, 1), (128, 1)))
    return {
        "inp": np.ascontiguousarray(inp_shard, dtype=np.float32),
        "w_in": np.ascontiguousarray(np.asarray(W["L0_attn_W_in"], dtype=np.float32)),
        "w_x": np.ascontiguousarray(np.asarray(W["L0_attn_W_x"], dtype=np.float32)),
        "w_dt": np.ascontiguousarray(np.asarray(W["L0_attn_W_dt"], dtype=np.float32)),
        "w_out": np.ascontiguousarray(np.asarray(W["L0_attn_W_out"], dtype=np.float32)),
        "ffn_w": np.ascontiguousarray(np.asarray(W["L0_ffn_weight"], dtype=np.float32)),
        "a_cols": a_cols.astype(np.float32),
        "d_cols": d_cols.astype(np.float32),
        "convw_cols": convw_cols.astype(np.float32),
        "convb_cols": convb_cols.astype(np.float32),
        "bdt_cols": bdt_cols.astype(np.float32),
        "ffnb_cols": ffnb_cols.astype(np.float32),
        "alpha_bc": alpha_bc.astype(np.float32),
        "ident_in": np.eye(128, dtype=np.float32),
        "ones_in": np.ones((1, 128), dtype=np.float32),
        "onescol_in": np.ones((128, 1), dtype=np.float32),
    }


def kernel(**inputs):
    full = inputs["L0_attn_inp"]
    BATCH, T = full.shape[0], full.shape[1]
    B = BATCH // NCORES
    nc = build_nc(B, T, 256)
    in_maps = [make_in_map(full[ci * B:(ci + 1) * B], inputs) for ci in range(NCORES)]
    res = run_bass_kernel_spmd(nc, in_maps, core_ids=list(range(NCORES)))
    return np.concatenate([np.asarray(r["out"]) for r in res.results], axis=0)


# version 30 (final)
# speedup vs baseline: 1.0768x; 1.0219x over previous
import numpy as np

import concourse.bass as bass
import concourse.mybir as mybir
from concourse.bass_utils import run_bass_kernel_spmd

F32 = mybir.dt.float32
BF16 = mybir.dt.bfloat16
AF = mybir.ActivationFunctionType
ALU = mybir.AluOpType
AX = mybir.AxisListType

D = 1024
DI = 2048
DS = 16
NCORES = 8
NJ = DI // 128
NK = D // 128


class Ev:
    __slots__ = ("sem", "val")

    def __init__(self, sem, val):
        self.sem = sem
        self.val = val


class TT:
    __slots__ = ("name", "w", "rs")

    def __init__(self, name):
        self.name = name
        self.w = None
        self.rs = []


class Rec:
    def __init__(self, sem):
        self.sem = sem
        self.n = 0
        self.marks = {}
        self.thunks = []

    def wait(self, ev):
        if ev is None:
            return
        key = id(ev.sem)
        if self.marks.get(key, 0) >= ev.val:
            return
        self.marks[key] = ev.val
        s, v = ev.sem, ev.val
        self.thunks.append(lambda e: e.wait_ge(s, v))

    def waits(self, deps):
        for d in deps:
            self.wait(d)

    def op(self, fn, deps=()):
        self.waits(deps)
        self.n += 1
        mysem, myval = self.sem, self.n
        self.thunks.append(lambda e: fn(e).then_inc(mysem, 1))
        return Ev(mysem, myval)


class KB:
    def __init__(self, nc):
        self.nc = nc
        self.dma_counts = {}
        self.pe = Rec(nc.alloc_semaphore("sem_pe"))
        self.dve = Rec(nc.alloc_semaphore("sem_dve"))
        self.act = Rec(nc.alloc_semaphore("sem_act"))
        self.pool = Rec(nc.alloc_semaphore("sem_pool"))
        self.sp = Rec(nc.alloc_semaphore("sem_sp"))

    def dma_sem(self, name):
        s = self.nc.alloc_semaphore(name)
        self.dma_counts[id(s)] = 0
        return s

    @staticmethod
    def _deps(reads, writes):
        deps = []
        for t in reads:
            if t.w is not None:
                deps.append(t.w)
        for t in writes:
            deps.extend(t.rs)
            if t.w is not None:
                deps.append(t.w)
        return deps

    @staticmethod
    def _record(ev, reads, writes):
        for t in reads:
            t.rs.append(ev)
        for t in writes:
            t.w = ev
            t.rs = []
        return ev

    def emit(self, eng, fn, reads=(), writes=()):
        ev = eng.op(fn, self._deps(reads, writes))
        return self._record(ev, reads, writes)

    def dma(self, eng, sem, fn, reads=(), writes=()):
        eng.waits(self._deps(reads, writes))
        self.dma_counts[id(sem)] += 16
        cnt = self.dma_counts[id(sem)]
        eng.thunks.append(lambda e: fn(e).then_inc(sem, 16))
        ev = Ev(sem, cnt)
        return self._record(ev, reads, writes)

    def mm_group(self, items, psum_writes):
        pe = self.pe
        first_deps = []
        for t in psum_writes:
            first_deps.extend(t.rs)
            if t.w is not None:
                first_deps.append(t.w)
        ev = None
        for i, (fn, rts) in enumerate(items):
            deps = list(first_deps) if i == 0 else []
            for t in rts:
                if t.w is not None:
                    deps.append(t.w)
            ev = pe.op(fn, deps)
            for t in rts:
                t.rs.append(ev)
        for t in psum_writes:
            t.w = ev
            t.rs = []
        return ev


class MultiAcc:

    def __init__(self, kb, tiles):
        self.kb = kb
        self.tiles = tiles
        self.started = set()
        self.evs = {}

    def mm(self, fn, reads, gi):
        deps = []
        if gi not in self.started:
            t = self.tiles[gi]
            deps.extend(t.rs)
            if t.w is not None:
                deps.append(t.w)
            self.started.add(gi)
        for t in reads:
            if t.w is not None:
                deps.append(t.w)
        ev = self.kb.pe.op(fn, deps)
        for t in reads:
            t.rs.append(ev)
        self.evs[gi] = ev
        return ev

    def finish(self):
        for gi in self.started:
            t = self.tiles[gi]
            t.w = self.evs[gi]
            t.rs = []



def build_nc(B, T, CH):
    NC_STEPS = T // CH
    NTT = CH // 128
    CS_MAX = CH + 2
    NSTEP = B * NC_STEPS
    nc = bass.Bass("TRN2", target_bir_lowering=False)
    kb = KB(nc)

    inp = nc.dram_tensor("inp", [B, T, D], F32, kind="ExternalInput").ap()
    w_in = nc.dram_tensor("w_in", [D, 2 * DI], F32, kind="ExternalInput").ap()
    w_x = nc.dram_tensor("w_x", [DI, 2 * DS], F32, kind="ExternalInput").ap()
    w_dt = nc.dram_tensor("w_dt", [DI, DI], F32, kind="ExternalInput").ap()
    w_out = nc.dram_tensor("w_out", [DI, D], F32, kind="ExternalInput").ap()
    ffn_w = nc.dram_tensor("ffn_w", [D, D], F32, kind="ExternalInput").ap()
    a_cols = nc.dram_tensor("a_cols", [128, NJ * DS], F32, kind="ExternalInput").ap()
    d_cols = nc.dram_tensor("d_cols", [128, NJ], F32, kind="ExternalInput").ap()
    convw_cols = nc.dram_tensor("convw_cols", [128, NJ * 4], F32, kind="ExternalInput").ap()
    convb_cols = nc.dram_tensor("convb_cols", [128, NJ], F32, kind="ExternalInput").ap()
    bdt_cols = nc.dram_tensor("bdt_cols", [128, NJ], F32, kind="ExternalInput").ap()
    ffnb_cols = nc.dram_tensor("ffnb_cols", [128, NK], F32, kind="ExternalInput").ap()
    alpha_bc = nc.dram_tensor("alpha_bc", [128, 1], F32, kind="ExternalInput").ap()
    ident_in = nc.dram_tensor("ident_in", [128, 128], F32, kind="ExternalInput").ap()
    ones_in = nc.dram_tensor("ones_in", [1, 128], F32, kind="ExternalInput").ap()
    onescol_in = nc.dram_tensor("onescol_in", [128, 1], F32, kind="ExternalInput").ap()
    out = nc.dram_tensor("out", [B, T, D], F32, kind="ExternalOutput").ap()
    win_h = nc.dram_tensor("win_h", [D, 2 * DI], BF16).ap()
    win_l = nc.dram_tensor("win_l", [D, 2 * DI], BF16).ap()
    wdt_h = nc.dram_tensor("wdt_h", [DI, DI], BF16).ap()
    wdt_l = nc.dram_tensor("wdt_l", [DI, DI], BF16).ap()
    wout_h = nc.dram_tensor("wout_h", [DI, D], BF16).ap()
    wout_l = nc.dram_tensor("wout_l", [DI, D], BF16).ap()
    wtT_scr = nc.dram_tensor("wtT_scr", [D, D], BF16).ap()

    S = lambda name, shape, dt=F32: nc.alloc_sbuf_tensor(name, shape, dt)

    x_tm = [S(f"x_tm{i}", [128, D]) for i in range(NTT)]
    ln_sc = S("ln_sc", [128, 16])
    bnst = S("bnst", [128, 12])
    XNW = 2 + CH
    xn_hl = [S(f"xnhl{j}", [128, 2 * XNW], BF16) for j in range(NK)]
    xnh = [t[:, 0:XNW] for t in xn_hl]
    xnl = [t[:, XNW:2 * XNW] for t in xn_hl]
    xi_hl = [S(f"xihl{j}", [128, 2 * CH], BF16) for j in range(NK)]
    xih = [t[:, 0:CH] for t in xi_hl]
    xil = [t[:, CH:2 * CH] for t in xi_hl]
    x_pre = [S(f"xpre{j}", [128, CH + 5]) for j in range(NJ)]
    conv_acc = [S(f"cacc{i}", [128, CS_MAX]) for i in range(2)]
    xf_hl = [S(f"xfhl{j}", [128, 2 * CS_MAX], BF16) for j in range(NJ)]
    xfh = [t[:, 0:CS_MAX] for t in xf_hl]
    xfl = [t[:, CS_MAX:2 * CS_MAX] for t in xf_hl]
    sz_buf = [S(f"sz{j}", [128, 2 + CH]) for j in range(NJ)]
    delta = [S(f"dl{j}", [128, CS_MAX]) for j in range(NJ)]
    wv = [S(f"wvv{j}", [128, CS_MAX]) for j in range(NJ)]
    y_t = [S(f"yy{j}", [128, CS_MAX]) for j in range(NJ)]
    y_bf = [S(f"ybf{j}", [128, 2 * CS_MAX], BF16) for j in range(NJ)]
    ybh = [t[:, 0:CS_MAX] for t in y_bf]
    ybl = [t[:, CS_MAX:2 * CS_MAX] for t in y_bf]
    NRING = 4
    h_rg = S("h_rg", [128, NRING * CS_MAX])
    a_t = [S(f"aa{i}", [128, CS_MAX]) for i in range(NRING)]
    b_t = [S(f"bb{i}", [128, CS_MAX]) for i in range(NRING)]
    cm_t = [S(f"cmm{i}", [128, CS_MAX]) for i in range(2)]
    NBREP = 3
    bcrep = [S(f"bcr{i}", [128, 2 * CS_MAX]) for i in range(NBREP)]
    brow = [S(f"brw{i}", [1, 2 * CS_MAX]) for i in range(2)]

    def hslot(j):
        return h_rg.ap()[:, (j % NRING) * CS_MAX:(j % NRING) * CS_MAX + CS_MAX]
    bc_sb = S("bc_sb", [32, CS_MAX])
    carries = S("carries", [128, NJ * DS])
    tails = S("tails", [128, 3 * NJ])
    s_sb = [[S(f"s_sb{p}_{o}", [128, CS_MAX]) for o in range(NK)] for p in range(2)]
    g_sb = [S(f"g_sb{o}", [128, CS_MAX]) for o in range(NK)]
    ostg = S("ostg0", [128, D])
    wb_h = [S(f"wbh{i}", [128, 512], BF16) for i in range(2)]
    wb_l = [S(f"wbl{i}", [128, 512], BF16) for i in range(2)]
    wx_h = S("wx_h", [128, NJ * 2 * DS], BF16)
    wx_l = S("wx_l", [128, NJ * 2 * DS], BF16)
    acol_sb = S("acol_sb", [128, NJ * DS])
    dcol_sb = S("dcol_sb", [128, NJ])
    cw_sb = S("cw_sb", [128, NJ * 4])
    cb_sb = S("cb_sb", [128, NJ])
    bdt_sb = S("bdt_sb", [128, NJ])
    ffnb_sb = S("ffnb_sb", [128, NK])
    alpha_sb = S("alpha_sb", [128, 1])
    ident_sb = S("ident_sb", [128, 128])
    ones_sb = S("ones_sb", [1, 128])
    onescol_sb = S("onescol_sb", [128, 1])
    thr_sb = S("thr_sb", [128, 2])
    eps_sb = S("eps_sb", [128, 1])
    fscr = x_tm[0]
    ftrn = x_tm[1]

    PS8 = [nc.alloc_psum_tensor(f"ps{i}", [128, 512], F32) for i in range(8)]
    P = PS8

    def pacc(i):
        return PS8[i].ap()

    t_x_tm = [TT(f"x_tm{i}") for i in range(NTT)]
    t_lnsc = TT("ln_sc"); t_bnst = TT("bnst")
    t_xnh = [TT(f"xnh{j}") for j in range(NK)]
    t_xnl = [TT(f"xnl{j}") for j in range(NK)]
    t_xih = [TT(f"xih{j}") for j in range(NK)]
    t_xil = [TT(f"xil{j}") for j in range(NK)]
    t_xpre = [TT(f"xpre{j}") for j in range(NJ)]
    t_cacc = [TT(f"cacc{i}") for i in range(2)]
    t_xfh = [TT(f"xfh{j}") for j in range(NJ)]
    t_xfl = [TT(f"xfl{j}") for j in range(NJ)]
    t_sz = [TT(f"sz{j}") for j in range(NJ)]
    t_dl = [TT(f"dl{j}") for j in range(NJ)]
    t_wv = [TT(f"wv{j}") for j in range(NJ)]
    t_y = [TT(f"y{j}") for j in range(NJ)]
    t_ybh = [TT(f"ybh{j}") for j in range(NJ)]
    t_ybl = [TT(f"ybl{j}") for j in range(NJ)]
    t_h = [TT(f"h{i}") for i in range(NRING)]
    t_a = [TT(f"a{i}") for i in range(NRING)]
    t_b = [TT(f"b{i}") for i in range(NRING)]
    t_cm = [TT(f"cm{i}") for i in range(2)]
    t_bcr = [TT(f"bcr{i}") for i in range(NBREP)]
    t_brow = [TT(f"brow{i}") for i in range(2)]
    t_bc = TT("bc")
    t_car = [[TT(f"car{s}_{g}") for g in range(NJ // 4)] for s in range(DS)]
    t_tails = TT("tails")
    t_s = [[TT(f"s{p}_{o}") for o in range(NK)] for p in range(2)]
    t_g = [TT(f"g{o}") for o in range(NK)]
    t_ostg = TT("ostg0")
    t_wbh = [TT(f"wbh{i}") for i in range(2)]
    t_wbl = [TT(f"wbl{i}") for i in range(2)]
    t_wx = TT("wx"); t_consts = TT("consts")
    t_thr = TT("thr"); t_fscr = t_x_tm[0]; t_ftrn = t_x_tm[1]
    t_P = [TT(f"P{i}") for i in range(8)]
    t_pacc = t_P
    t_out_hbm = TT("out_hbm")
    t_w_win = TT("w_win"); t_w_wdt = TT("w_wdt")
    t_w_wout = TT("w_wout"); t_w_ffn = TT("w_ffn")
    t_chain_h = TT("chain_h"); t_chain_l = TT("chain_l")
    t_fs4 = [TT(f"fs4_{i}") for i in range(4)]

    sem_const = kb.dma_sem("d_const")
    sem_xtm = [kb.dma_sem(f"d_xtm{i}") for i in range(NTT)]
    sem_wh = [kb.dma_sem(f"d_wh{i}") for i in range(2)]
    sem_wl = [kb.dma_sem(f"d_wl{i}") for i in range(2)]
    sem_row = [kb.dma_sem(f"d_row{i}") for i in range(2)]
    sem_oacc = kb.dma_sem("d_oacc0")
    sem_bulk = kb.dma_sem("d_bulk")
    sem_fprep = kb.dma_sem("d_fprep")
    sem_fprep4 = [kb.dma_sem(f"d_fp4_{i}") for i in range(4)]
    sem_fph = kb.dma_sem("d_fph")
    sem_fpl = kb.dma_sem("d_fpl")
    sem_wx = kb.dma_sem("d_wx")

    pe, dve, act, pool, sp = kb.pe, kb.dve, kb.act, kb.pool, kb.sp

    kb.dma(sp, sem_bulk, lambda e: e.dma_start(out=out, in_=inp),
           reads=(), writes=(t_out_hbm,))

    def cdma(fn):
        kb.dma(sp, sem_const, fn, reads=(), writes=(t_consts,))

    cdma(lambda e: e.dma_start(out=acol_sb[:], in_=a_cols))
    cdma(lambda e: e.dma_start(out=dcol_sb[:], in_=d_cols))
    cdma(lambda e: e.dma_start(out=cw_sb[:], in_=convw_cols))
    cdma(lambda e: e.dma_start(out=cb_sb[:], in_=convb_cols))
    cdma(lambda e: e.dma_start(out=bdt_sb[:], in_=bdt_cols))
    cdma(lambda e: e.dma_start(out=ffnb_sb[:], in_=ffnb_cols))
    cdma(lambda e: e.dma_start(out=alpha_sb[:], in_=alpha_bc))
    cdma(lambda e: e.dma_start(out=ident_sb[:], in_=ident_in))
    cdma(lambda e: e.dma_start(out=ones_sb[:], in_=ones_in))
    cdma(lambda e: e.dma_start(out=onescol_sb[:], in_=onescol_in))

    kb.emit(act, lambda e: e.activation(eps_sb[:], onescol_sb[:], AF.Copy,
                                        bias=0.0, scale=1e-5),
            reads=(t_consts,), writes=(t_consts,))

    for k in range(NJ):
        kb.dma(sp, sem_wx,
               lambda e, k=k: e.dma_start(out=fscr[:, 0:2 * DS],
                                          in_=w_x[k * 128:(k + 1) * 128, :]),
               reads=(), writes=(t_fscr,))
        kb.emit(act, lambda e, k=k: e.copy(wx_h[:, k * 2 * DS:(k + 1) * 2 * DS],
                                           fscr[:, 0:2 * DS]),
                reads=(t_fscr,), writes=(t_wx,))
        kb.emit(dve, lambda e, k=k: e.tensor_sub(wx_l[:, k * 2 * DS:(k + 1) * 2 * DS],
                                                 fscr[:, 0:2 * DS],
                                                 wx_h[:, k * 2 * DS:(k + 1) * 2 * DS]),
                reads=(t_fscr, t_wx), writes=(t_wx,))

    pctr = [0]

    def prep_weight(src_ap, dst_h, dst_l, rows, cols, tok):
        NR, NCP = rows // 128, cols // 512
        for r in range(NR):
            for cp in range(NCP):
                first = r == 0 and cp == 0
                last = r == NR - 1 and cp == NCP - 1
                c0 = cp * 512
                p = pctr[0]
                pctr[0] += 1
                fi = p % 4
                f, fc, tf = x_tm[fi // 2], (fi % 2) * 512, t_fs4[fi]
                lwr = (tf, t_x_tm[fi // 2]) if p < 4 else (tf,)
                yi = (2 * p) % 16
                hi_ap = y_bf[yi].ap()[:, 0:512]
                lo_ap = y_bf[yi + 1].ap()[:, 0:512]
                th = (t_ybh[yi], t_ybl[yi])
                tl = (t_ybh[yi + 1], t_ybl[yi + 1])
                kb.dma(sp, sem_fprep4[fi],
                       lambda e, r=r, c0=c0, src_ap=src_ap, f=f, fc=fc: e.dma_start(
                           out=f[:, fc:fc + 512],
                           in_=src_ap[r * 128:(r + 1) * 128, c0:c0 + 512]),
                       reads=(), writes=lwr)
                kb.emit(act, lambda e, f=f, fc=fc, hi_ap=hi_ap: e.copy(
                    hi_ap, f[:, fc:fc + 512]),
                    reads=(tf,), writes=th)
                kb.emit(dve, lambda e, f=f, fc=fc, hi_ap=hi_ap, lo_ap=lo_ap:
                        e.tensor_sub(lo_ap, f[:, fc:fc + 512], hi_ap),
                        reads=(tf,) + th, writes=tl)
                kb.dma(pool, sem_fph,
                       lambda e, r=r, c0=c0, dst_h=dst_h, hi_ap=hi_ap: e.dma_start(
                           out=dst_h[r * 128:(r + 1) * 128, c0:c0 + 512], in_=hi_ap),
                       reads=th + ((t_chain_h,) if first else ()),
                       writes=(tok,) + ((t_chain_h,) if last else ()))
                kb.dma(pool, sem_fpl,
                       lambda e, r=r, c0=c0, dst_l=dst_l, lo_ap=lo_ap: e.dma_start(
                           out=dst_l[r * 128:(r + 1) * 128, c0:c0 + 512], in_=lo_ap),
                       reads=tl + ((t_chain_l,) if first else ()),
                       writes=(tok,) + ((t_chain_l,) if last else ()))

    prep_weight(w_in, win_h, win_l, D, 2 * DI, t_w_win)
    prep_weight(w_dt, wdt_h, wdt_l, DI, DI, t_w_wdt)

    prep_weight(w_out, wout_h, wout_l, DI, D, t_w_wout)

    for k in range(NK):
        kb.dma(sp, sem_fprep,
               lambda e, k=k: e.dma_start(out=fscr[:], in_=ffn_w[k * 128:(k + 1) * 128, :]),
               reads=(), writes=(t_fscr, t_fs4[0], t_fs4[1]))
        kb.emit(dve, lambda e, k=k: e.tensor_reduce(ln_sc.ap()[:, k:k + 1], fscr.ap(),
                                                    AX.X, ALU.add, apply_absolute_value=True),
                reads=(t_fscr,), writes=(t_lnsc,))
    kb.emit(pe, lambda e: e.matmul(P[3].ap()[0:1, 0:8], onescol_sb[:], ln_sc.ap()[:, 0:8],
                                   start=True, stop=True),
            reads=(t_lnsc, t_consts), writes=(t_P[3],))
    kb.emit(dve, lambda e: e.tensor_reduce(ln_sc.ap()[0:1, 8:9], P[3].ap()[0:1, 0:8],
                                           AX.X, ALU.add),
            reads=(t_P[3],), writes=(t_lnsc,))
    kb.emit(pe, lambda e: e.matmul(P[3].ap()[:, 0:1], ones_sb[:], ln_sc.ap()[0:1, 8:9],
                                   start=True, stop=True),
            reads=(t_lnsc, t_consts), writes=(t_P[3],))
    kb.emit(act, lambda e: e.activation(thr_sb[:, 0:1], P[3].ap()[:, 0:1], AF.Copy,
                                        bias=0.0, scale=0.7 / (D * D)),
            reads=(t_P[3],), writes=(t_thr,))
    kb.emit(act, lambda e: e.activation(thr_sb[:, 1:2], thr_sb[:, 0:1], AF.Copy,
                                        bias=0.0, scale=-1.0),
            reads=(t_thr,), writes=(t_thr,))
    for r in range(NK):
        kb.dma(sp, sem_fprep,
               lambda e, r=r: e.dma_start(out=fscr[:], in_=ffn_w[r * 128:(r + 1) * 128, :]),
               reads=(), writes=(t_fscr, t_fs4[0], t_fs4[1]))
        kb.emit(dve, lambda e: e.tensor_scalar(ftrn.ap(), fscr.ap(), thr_sb[:, 0:1], None, ALU.is_gt),
                reads=(t_fscr, t_thr), writes=(t_ftrn, t_fs4[2], t_fs4[3]))
        kb.emit(dve, lambda e: e.tensor_scalar(ostg.ap(), fscr.ap(), thr_sb[:, 1:2], None, ALU.is_lt),
                reads=(t_fscr, t_thr), writes=(t_ostg,))
        kb.emit(dve, lambda e: e.tensor_sub(ftrn.ap(), ftrn.ap(), ostg.ap()),
                reads=(t_ostg, t_ftrn), writes=(t_ftrn,))
        grp = []
        for j in range(NK):
            bank = PS8[6] if j < 4 else PS8[7]
            def fn(e, j=j, bank=bank, first=(j % 4 == 0), last=(j == NK - 1)):
                return e.matmul(bank.ap()[:, (j % 4) * 128:(j % 4 + 1) * 128],
                                ftrn.ap()[:, j * 128:(j + 1) * 128], ident_sb[:],
                                start=first, stop=last, is_transpose=True,
                                skip_group_check=True)
            grp.append((fn, [t_ftrn, t_consts]))
        kb.mm_group(grp, (t_P[6], t_P[7]))
        kb.emit(act, lambda e: e.copy(wb_h[0][:], PS8[6].ap()),
                reads=(t_P[6],), writes=(t_wbh[0],))
        for j in range(4):
            kb.dma(pool, sem_fph,
                   lambda e, j=j, r=r: e.dma_start(
                       out=wtT_scr[j * 128:(j + 1) * 128, r * 128:(r + 1) * 128],
                       in_=wb_h[0][:, j * 128:(j + 1) * 128]),
                   reads=(t_wbh[0],) + ((t_chain_h,) if r == 0 and j == 0 else ()),
                   writes=(t_w_ffn,))
        kb.emit(act, lambda e: e.copy(wb_h[1][:], PS8[7].ap()),
                reads=(t_P[7],), writes=(t_wbh[1],))
        for j in range(4, 8):
            kb.dma(pool, sem_fph,
                   lambda e, j=j, r=r: e.dma_start(
                       out=wtT_scr[j * 128:(j + 1) * 128, r * 128:(r + 1) * 128],
                       in_=wb_h[1][:, (j - 4) * 128:(j - 3) * 128]),
                   reads=(t_wbh[1],), writes=(t_w_ffn,))



    def split_cols(src_fn, hi_ap, lo_ap, reads, whi, wlo):
        kb.emit(dve, lambda e: e.tensor_copy(hi_ap, src_fn()), reads=reads, writes=(whi,))
        kb.emit(dve, lambda e: e.tensor_sub(lo_ap, src_fn(), hi_ap),
                reads=tuple(reads) + (whi,), writes=(wlo,))

    def hl_rhs(hl_handle, W, off, colsS):
        return hl_handle.ap().rearrange("p (s w) -> p s w", w=W)[:, :, off:off + colsS]

    def mm_hl3(ma, gi, whi_ap, wlo_ap, hl_handle, W, off, colsS,
               reads_hi, reads_lo, t_x, first, last, rows=128):
        if 2 * colsS <= 512:
            rhs = hl_rhs(hl_handle, W, off, colsS)
            def f1(e, whi_ap=whi_ap, rhs=rhs, gi=gi, colsS=colsS, first=first, rows=rows):
                return e.matmul(pacc(gi)[0:rows, 0:2 * colsS], whi_ap, rhs,
                                start=first, stop=False, skip_group_check=True)
            ma.mm(f1, list(reads_hi) + list(t_x), gi)
            hi_s = hl_handle.ap()[:, off:off + colsS]
            def f2(e, wlo_ap=wlo_ap, hi_s=hi_s, gi=gi, colsS=colsS, last=last, rows=rows):
                return e.matmul(pacc(gi)[0:rows, 0:colsS], wlo_ap, hi_s,
                                start=False, stop=last, skip_group_check=True)
            ma.mm(f2, list(reads_lo) + list(t_x), gi)
        else:
            hi_s = hl_handle.ap()[:, off:off + colsS]
            lo_s = hl_handle.ap()[:, W + off:W + off + colsS]
            def f1(e, whi_ap=whi_ap, hi_s=hi_s, gi=gi, colsS=colsS, first=first, rows=rows):
                return e.matmul(pacc(gi)[0:rows, 0:colsS], whi_ap, hi_s,
                                start=first, stop=False, skip_group_check=True)
            ma.mm(f1, list(reads_hi) + list(t_x), gi)
            def f2(e, whi_ap=whi_ap, lo_s=lo_s, gi=gi, colsS=colsS, rows=rows):
                return e.matmul(pacc(gi)[0:rows, 0:colsS], whi_ap, lo_s,
                                start=False, stop=False, skip_group_check=True)
            ma.mm(f2, list(reads_hi) + list(t_x), gi)
            def f3(e, wlo_ap=wlo_ap, hi_s=hi_s, gi=gi, colsS=colsS, last=last, rows=rows):
                return e.matmul(pacc(gi)[0:rows, 0:colsS], wlo_ap, hi_s,
                                start=False, stop=last, skip_group_check=True)
            ma.mm(f3, list(reads_lo) + list(t_x), gi)

    def mm_hl1(ma, gi, w_ap, hl_handle, W, off, colsS, reads_w, t_x, first, last):
        if 2 * colsS <= 512:
            rhs = hl_rhs(hl_handle, W, off, colsS)
            def f1(e, w_ap=w_ap, rhs=rhs, gi=gi, colsS=colsS, first=first, last=last):
                return e.matmul(pacc(gi)[:, 0:2 * colsS], w_ap, rhs,
                                start=first, stop=last, skip_group_check=True)
            ma.mm(f1, list(reads_w) + list(t_x), gi)
        else:
            hi_s = hl_handle.ap()[:, off:off + colsS]
            lo_s = hl_handle.ap()[:, W + off:W + off + colsS]
            def f1(e, w_ap=w_ap, hi_s=hi_s, gi=gi, colsS=colsS, first=first):
                return e.matmul(pacc(gi)[:, 0:colsS], w_ap, hi_s,
                                start=first, stop=False, skip_group_check=True)
            ma.mm(f1, list(reads_w) + list(t_x), gi)
            def f2(e, w_ap=w_ap, lo_s=lo_s, gi=gi, colsS=colsS, last=last):
                return e.matmul(pacc(gi)[:, 0:colsS], w_ap, lo_s,
                                start=False, stop=last, skip_group_check=True)
            ma.mm(f2, list(reads_w) + list(t_x), gi)

    def psum2_ap(gi, colsS, rows=128):
        return PS8[gi].ap()[0:rows, 0:2 * colsS].rearrange("p (s w) -> p w s", s=2)

    def evac_sum2(eng, dst_fn, gi, colsS, extra_reads=(), writes=()):
        if 2 * colsS <= 512:
            kb.emit(eng, lambda e: e.tensor_reduce(dst_fn(), psum2_ap(gi, colsS),
                                                   AX.X, ALU.add),
                    reads=(t_pacc[gi],) + tuple(extra_reads), writes=writes)
        else:
            kb.emit(eng, lambda e: e.tensor_copy(dst_fn(), pacc(gi)[:, 0:colsS]),
                    reads=(t_pacc[gi],) + tuple(extra_reads), writes=writes)

    def evac_add2(o, par, gi, colsS):
        halves = [0, colsS] if 2 * colsS <= 512 else [0]
        for a in halves:
            kb.emit(dve, lambda e, o=o, par=par, gi=gi, a=a, colsS=colsS:
                    e.scalar_tensor_tensor(
                        s_sb[par][o][:, 0:colsS], pacc(gi)[:, a:a + colsS],
                        onescol_sb[:, 0:1], s_sb[par][o][:, 0:colsS],
                        ALU.mult, ALU.add),
                    reads=(t_pacc[gi], t_s[par][o], t_consts), writes=(t_s[par][o],))


    def params(n):
        b, c = divmod(n, NC_STEPS)
        s0 = max(0, c * CH - 2)
        e0 = T if c == NC_STEPS - 1 else (c + 1) * CH - 2
        return b, c, s0, e0 - s0, s0 - c * CH + 2, (c == NC_STEPS - 1)

    def pieces_input_ln(n):
        b, c = divmod(n, NC_STEPS)
        ps = []

        def p_dma(b=b, c=c):
            for i in range(NTT):
                kb.dma(sp, sem_xtm[i],
                       lambda e, i=i, b=b, c=c: e.dma_start(
                           out=x_tm[i][:],
                           in_=inp[b, c * CH + i * 128: c * CH + (i + 1) * 128, :]),
                       reads=(), writes=(t_x_tm[i],))
        ps.append(p_dma)

        for i in range(NTT):
            def p_ln(i=i):
                cb8 = 8 * i
                kb.emit(dve, lambda e, i=i: e.bn_stats(bnst.ap()[:, 0:6], x_tm[i][:, 0:512]),
                        reads=(t_x_tm[i],), writes=(t_bnst,))
                kb.emit(dve, lambda e, i=i: e.bn_stats(bnst.ap()[:, 6:12], x_tm[i][:, 512:1024]),
                        reads=(t_x_tm[i],), writes=(t_bnst,))
                kb.emit(dve, lambda e, cb8=cb8: e.bn_aggr(ln_sc.ap()[:, cb8:cb8 + 2], bnst.ap()),
                        reads=(t_bnst,), writes=(t_lnsc,))
                kb.emit(act, lambda e, cb8=cb8: e.activation(ln_sc.ap()[:, cb8 + 2:cb8 + 3],
                                                             ln_sc.ap()[:, cb8:cb8 + 1],
                                                             AF.Copy, bias=0.0, scale=-1.0),
                        reads=(t_lnsc,), writes=(t_lnsc,))
                kb.emit(act, lambda e, cb8=cb8: e.activation(ln_sc.ap()[:, cb8 + 6:cb8 + 7],
                                                             ln_sc.ap()[:, cb8 + 1:cb8 + 2],
                                                             AF.Sqrt, bias=eps_sb[:], scale=1.0),
                        reads=(t_lnsc, t_consts), writes=(t_lnsc,))
                kb.emit(dve, lambda e, cb8=cb8: e.reciprocal(ln_sc.ap()[:, cb8 + 3:cb8 + 4],
                                                             ln_sc.ap()[:, cb8 + 6:cb8 + 7]),
                        reads=(t_lnsc,), writes=(t_lnsc,))
                kb.emit(dve, lambda e, i=i, cb8=cb8: e.tensor_scalar(
                    x_tm[i][:], x_tm[i][:],
                    ln_sc.ap()[:, cb8 + 2:cb8 + 3], ln_sc.ap()[:, cb8 + 3:cb8 + 4],
                    ALU.add, op1=ALU.mult),
                    reads=(t_lnsc, t_x_tm[i]), writes=(t_x_tm[i],))
                kb.emit(dve, lambda e, i=i, cb8=cb8: e.tensor_reduce(
                    ln_sc.ap()[:, cb8 + 4:cb8 + 5], x_tm[i][:], AX.X, ALU.add),
                    reads=(t_x_tm[i],), writes=(t_lnsc,))
                kb.emit(act, lambda e, cb8=cb8: e.activation(
                    ln_sc.ap()[:, cb8 + 5:cb8 + 6], ln_sc.ap()[:, cb8 + 4:cb8 + 5],
                    AF.Copy, bias=0.0, scale=-1.0 / D),
                    reads=(t_lnsc,), writes=(t_lnsc,))
            ps.append(p_ln)
        return ps

    def pieces_transposes(n):
        b, c, s0, colsS, base2, last = params(n)
        ps = []
        for j in range(NK):
            def p(j=j, c=c):
                if c > 0:
                    kb.emit(act, lambda e, j=j: e.copy(xnh[j][:, 0:2], xnh[j][:, CH:CH + 2]),
                            reads=(t_xnh[j],), writes=(t_xnh[j],))
                    kb.emit(pool, lambda e, j=j: e.tensor_copy(xnl[j][:, 0:2], xnl[j][:, CH:CH + 2]),
                            reads=(t_xnl[j],), writes=(t_xnl[j],))
                grp = []
                for i in range(NTT):
                    def fn(e, j=j, i=i, first=(i == 0), last_=(i == NTT - 1)):
                        return e.matmul(P[5].ap()[:, i * 128:(i + 1) * 128],
                                        x_tm[i][:, j * 128:(j + 1) * 128], ident_sb[:],
                                        start=first, stop=last_, is_transpose=True,
                                        skip_group_check=True)
                    grp.append((fn, [t_x_tm[i], t_consts]))
                kb.mm_group(grp, (t_P[5],))
                split_cols(lambda: P[5].ap()[:, 0:CH],
                           xnh[j][:, 2:2 + CH], xnl[j][:, 2:2 + CH],
                           (t_P[5],), t_xnh[j], t_xnl[j])
            ps.append(p)

        def p_msub():
            for i in range(NTT):
                cb8 = 8 * i
                kb.emit(dve, lambda e, i=i, cb8=cb8: e.tensor_scalar(
                    x_tm[i][:], x_tm[i][:], ln_sc.ap()[:, cb8 + 5:cb8 + 6], None, ALU.add),
                    reads=(t_lnsc, t_x_tm[i]), writes=(t_x_tm[i],))
        ps.append(p_msub)

        for j in range(NK):
            def p(j=j):
                grp = []
                for i in range(NTT):
                    def fn(e, j=j, i=i, first=(i == 0), last_=(i == NTT - 1)):
                        return e.matmul(P[5].ap()[:, i * 128:(i + 1) * 128],
                                        x_tm[i][:, j * 128:(j + 1) * 128], ident_sb[:],
                                        start=first, stop=last_, is_transpose=True,
                                        skip_group_check=True)
                    grp.append((fn, [t_x_tm[i], t_consts]))
                kb.mm_group(grp, (t_P[5],))
                split_cols(lambda: P[5].ap()[:, 0:CH],
                           xih[j][:], xil[j][:],
                           (t_P[5],), t_xih[j], t_xil[j])
            ps.append(p)
        return ps

    def pieces_xz_x(n):
        ps = []
        for blk in range(4):
            F0 = blk * 4
            st = {}

            def p_k(k, F0=F0, st=st):
                if 'ma' not in st:
                    st['ma'] = MultiAcc(kb, t_P)
                ma = st['ma']
                slot = k % 2
                kb.dma(sp, sem_wh[slot],
                       lambda e, k=k, F0=F0, slot=slot: e.dma_start(
                           out=wb_h[slot][:],
                           in_=win_h[k * 128:(k + 1) * 128, F0 * 128:F0 * 128 + 512]),
                       reads=(t_w_win,), writes=(t_wbh[slot],))
                kb.dma(sp, sem_wl[slot],
                       lambda e, k=k, F0=F0, slot=slot: e.dma_start(
                           out=wb_l[slot][:],
                           in_=win_l[k * 128:(k + 1) * 128, F0 * 128:F0 * 128 + 512]),
                       reads=(t_w_win,), writes=(t_wbl[slot],))
                for fi in range(4):
                    mm_hl3(ma, fi,
                           wb_h[slot][:, fi * 128:(fi + 1) * 128],
                           wb_l[slot][:, fi * 128:(fi + 1) * 128],
                           xi_hl[k], CH, 0, CH,
                           (t_wbh[slot],), (t_wbl[slot],), (t_xih[k], t_xil[k]),
                           first=(k == 0), last=(k == NK - 1))
            for k in range(NK):
                ps.append(lambda k=k, p_k=p_k: p_k(k))

            def p_evac(F0=F0, st=st):
                st['ma'].finish()
                for fi in range(4):
                    jj = F0 + fi
                    evac_sum2(dve, lambda jj=jj: x_pre[jj][:, 3:3 + CH], fi, CH,
                              writes=(t_xpre[jj],))
            ps.append(p_evac)
        return ps

    def pieces_z(n):
        b, c, s0, colsS, base2, last = params(n)
        ps = []
        for blk in range(4):
            F0 = NJ + blk * 4
            st = {}

            def p_k(k, F0=F0, st=st):
                if 'ma' not in st:
                    st['ma'] = MultiAcc(kb, t_P)
                ma = st['ma']
                slot = k % 2
                kb.dma(sp, sem_wh[slot],
                       lambda e, k=k, F0=F0, slot=slot: e.dma_start(
                           out=wb_h[slot][:],
                           in_=win_h[k * 128:(k + 1) * 128, F0 * 128:F0 * 128 + 512]),
                       reads=(t_w_win,), writes=(t_wbh[slot],))
                kb.dma(sp, sem_wl[slot],
                       lambda e, k=k, F0=F0, slot=slot: e.dma_start(
                           out=wb_l[slot][:],
                           in_=win_l[k * 128:(k + 1) * 128, F0 * 128:F0 * 128 + 512]),
                       reads=(t_w_win,), writes=(t_wbl[slot],))
                for fi in range(4):
                    mm_hl3(ma, fi,
                           wb_h[slot][:, fi * 128:(fi + 1) * 128],
                           wb_l[slot][:, fi * 128:(fi + 1) * 128],
                           xi_hl[k], CH, 0, CH,
                           (t_wbh[slot],), (t_wbl[slot],), (t_xih[k], t_xil[k]),
                           first=(k == 0), last=(k == NK - 1))
            for k in range(NK):
                ps.append(lambda k=k, p_k=p_k: p_k(k))

            def p_evac(F0=F0, st=st, c=c):
                st['ma'].finish()
                for fi in range(4):
                    jj = F0 - NJ + fi
                    if c > 0:
                        kb.emit(act, lambda e, jj=jj: e.copy(sz_buf[jj][:, 0:2],
                                                             sz_buf[jj][:, CH:CH + 2]),
                                reads=(t_sz[jj],), writes=(t_sz[jj],))
                    stg2, tstg2 = conv_acc[fi % 2], t_cacc[fi % 2]
                    evac_sum2(dve, lambda stg2=stg2: stg2.ap()[:, 0:CH], fi, CH,
                              writes=(tstg2,))
                    kb.emit(act, lambda e, jj=jj, stg2=stg2: e.activation(
                        sz_buf[jj][:, 2:2 + CH], stg2.ap()[:, 0:CH], AF.Silu,
                        bias=0.0, scale=1.0),
                        reads=(tstg2,), writes=(t_sz[jj],))
            ps.append(p_evac)
        return ps

    def pieces_conv(n):
        b, c, s0, colsS, base2, last = params(n)
        off0 = base2
        ps = []
        if c == 0:
            ps.append(lambda: kb.emit(dve, lambda e: e.memset(tails.ap(), 0.0),
                                      reads=(), writes=(t_tails,)))
        for jj in range(NJ):
            def p(jj=jj, off0=off0, colsS=colsS, last=last):
                kb.emit(act, lambda e, jj=jj: e.copy(x_pre[jj][:, 0:3],
                                                     tails.ap()[:, jj * 3:jj * 3 + 3]),
                        reads=(t_tails,), writes=(t_xpre[jj],))
                if last:
                    kb.emit(dve, lambda e, jj=jj: e.memset(x_pre[jj][:, 3 + CH:5 + CH], 0.0),
                            reads=(), writes=(t_xpre[jj],))
                acc, tacc = conv_acc[jj % 2], t_cacc[jj % 2]
                kb.emit(dve, lambda e, jj=jj, acc=acc, off0=off0, colsS=colsS:
                        e.tensor_scalar(
                            acc.ap()[:, 0:colsS],
                            x_pre[jj][:, off0: off0 + colsS],
                            cw_sb[:, jj * 4: jj * 4 + 1], None, ALU.mult),
                        reads=(t_xpre[jj], t_consts), writes=(tacc,))
                for k4 in range(1, 4):
                    kb.emit(dve, lambda e, jj=jj, acc=acc, k4=k4, off0=off0, colsS=colsS:
                            e.scalar_tensor_tensor(
                                acc.ap()[:, 0:colsS],
                                x_pre[jj][:, off0 + k4: off0 + k4 + colsS],
                                cw_sb[:, jj * 4 + k4: jj * 4 + k4 + 1],
                                acc.ap()[:, 0:colsS], ALU.mult, ALU.add),
                            reads=(t_xpre[jj], t_consts, tacc), writes=(tacc,))
                if not last:
                    kb.emit(act, lambda e, jj=jj: e.copy(tails.ap()[:, jj * 3:jj * 3 + 3],
                                                         x_pre[jj][:, CH:CH + 3]),
                            reads=(t_xpre[jj],), writes=(t_tails,))
                kb.emit(act, lambda e, jj=jj, acc=acc, colsS=colsS: e.activation(
                    x_pre[jj][:, 0:colsS], acc.ap()[:, 0:colsS], AF.Silu,
                    bias=cb_sb[:, jj:jj + 1], scale=1.0),
                    reads=(tacc, t_consts), writes=(t_xpre[jj],))
                split_cols(lambda jj=jj, colsS=colsS: x_pre[jj][:, 0:colsS],
                           xfh[jj][:, 0:colsS], xfl[jj][:, 0:colsS],
                           (t_xpre[jj],), t_xfh[jj], t_xfl[jj])
            ps.append(p)
        return ps

    def pieces_delta(n):
        b, c, s0, colsS, base2, last = params(n)
        blks = []
        for blk in range(4):
            j0 = blk * 4
            with_wx = blk == 0
            st = {}
            ps = []

            def p_k(k, j0=j0, st=st, colsS=colsS, with_wx=with_wx):
                if 'ma' not in st:
                    st['ma'] = MultiAcc(kb, t_P)
                ma = st['ma']
                slot = k % 2
                kb.dma(sp, sem_wh[slot],
                       lambda e, k=k, slot=slot, j0=j0: e.dma_start(
                           out=wb_h[slot][:],
                           in_=wdt_h[k * 128:(k + 1) * 128, j0 * 128:j0 * 128 + 512]),
                       reads=(t_w_wdt,), writes=(t_wbh[slot],))
                kb.dma(sp, sem_wl[slot],
                       lambda e, k=k, slot=slot, j0=j0: e.dma_start(
                           out=wb_l[slot][:],
                           in_=wdt_l[k * 128:(k + 1) * 128, j0 * 128:j0 * 128 + 512]),
                       reads=(t_w_wdt,), writes=(t_wbl[slot],))
                for ji in range(4):
                    mm_hl3(ma, ji,
                           wb_h[slot][:, ji * 128:(ji + 1) * 128],
                           wb_l[slot][:, ji * 128:(ji + 1) * 128],
                           xf_hl[k], CS_MAX, 0, colsS,
                           (t_wbh[slot],), (t_wbl[slot],), (t_xfh[k], t_xfl[k]),
                           first=(k == 0), last=(k == NJ - 1))
                if with_wx:
                    mm_hl3(ma, 4,
                           wx_h[:, k * 2 * DS:(k + 1) * 2 * DS],
                           wx_l[:, k * 2 * DS:(k + 1) * 2 * DS],
                           xf_hl[k], CS_MAX, 0, colsS,
                           (t_wx,), (t_wx,), (t_xfh[k], t_xfl[k]),
                           first=(k == 0), last=(k == NJ - 1), rows=32)
            for k in range(NJ):
                ps.append(lambda k=k, p_k=p_k: p_k(k))

            def p_evac(j0=j0, st=st, colsS=colsS, with_wx=with_wx):
                st['ma'].finish()
                for ji in range(4):
                    j = j0 + ji
                    evac_sum2(dve, lambda j=j, colsS=colsS: delta[j][:, 0:colsS],
                              ji, colsS, writes=(t_dl[j],))
                if with_wx:
                    if 2 * colsS <= 512:
                        kb.emit(dve, lambda e, colsS=colsS: e.tensor_reduce(
                            bc_sb[0:32, 0:colsS], psum2_ap(4, colsS, rows=32),
                            AX.X, ALU.add),
                            reads=(t_pacc[4],), writes=(t_bc,))
                    else:
                        kb.emit(dve, lambda e, colsS=colsS: e.tensor_copy(
                            bc_sb[0:32, 0:colsS], pacc(4)[0:32, 0:colsS]),
                            reads=(t_pacc[4],), writes=(t_bc,))
            ps.append(p_evac)
            blks.append(ps)
        return blks

    def em_post_q(n, qb):
        b, c, s0, colsS, base2, last = params(n)
        js = list(range(qb * 4, (qb + 1) * 4))
        for j in js:
            kb.emit(act, lambda e, j=j, colsS=colsS: e.activation(
                wv[j][:, 0:colsS], delta[j][:, 0:colsS], AF.Sigmoid,
                bias=bdt_sb[:, j:j + 1], scale=1.0),
                reads=(t_dl[j], t_consts), writes=(t_wv[j],))
        for j in js:
            kb.emit(act, lambda e, j=j, colsS=colsS: e.activation(
                wv[j][:, 0:colsS], wv[j][:, 0:colsS], AF.Ln, bias=0.0, scale=1.0),
                reads=(t_wv[j],), writes=(t_wv[j],))
        for j in js:
            kb.emit(dve, lambda e, j=j, colsS=colsS: e.scalar_tensor_tensor(
                delta[j][:, 0:colsS], delta[j][:, 0:colsS],
                bdt_sb[:, j:j + 1], wv[j][:, 0:colsS],
                ALU.add, ALU.subtract),
                reads=(t_wv[j], t_dl[j], t_consts), writes=(t_dl[j],))
        for j in js:
            kb.emit(pool, lambda e, j=j, colsS=colsS: e.tensor_add(
                y_t[j][:, 0:colsS], xfh[j][:, 0:colsS], xfl[j][:, 0:colsS]),
                reads=(t_xfh[j], t_xfl[j]), writes=(t_y[j],))
            kb.emit(pool, lambda e, j=j, colsS=colsS: e.tensor_mul(
                wv[j][:, 0:colsS], delta[j][:, 0:colsS], y_t[j][:, 0:colsS]),
                reads=(t_dl[j], t_y[j]), writes=(t_wv[j],))
            kb.emit(act, lambda e, j=j, colsS=colsS: e.activation(
                y_t[j][:, 0:colsS], y_t[j][:, 0:colsS], AF.Identity,
                bias=0.0, scale=dcol_sb[:, j:j + 1]),
                reads=(t_y[j], t_consts), writes=(t_y[j],))

    def em_prefill(n):
        b, c, s0, colsS, base2, last = params(n)
        par = n % 2
        for o in range(NK):
            kb.emit(pool, lambda e, o=o, par=par, base2=base2, colsS=colsS: e.tensor_add(
                s_sb[par][o][:, 0:colsS],
                xnh[o][:, base2:base2 + colsS], xnl[o][:, base2:base2 + colsS]),
                reads=(t_xnh[o], t_xnl[o]), writes=(t_s[par][o],))
        if c == 0:
            kb.emit(dve, lambda e: e.memset(carries.ap(), 0.0), reads=(),
                    writes=tuple(t for row in t_car for t in row))

    BCBANKS = (4, 6, 7)

    def em_bcast(n, jb, s):
        b, c, s0, colsS, base2, last = params(n)
        si = jb * DS + s
        rb = si % 2
        sb_i = si % NBREP
        bk = BCBANKS[si % 3]
        kb.dma(sp, sem_row[rb],
               lambda e, s=s, rb=rb, colsS=colsS: e.dma_start(
                   out=brow[rb][0:1, 0:colsS], in_=bc_sb[s:s + 1, 0:colsS]),
               reads=(t_bc,), writes=(t_brow[rb],))
        if 2 * colsS <= 512:
            kb.dma(sp, sem_row[rb],
                   lambda e, s=s, rb=rb, colsS=colsS: e.dma_start(
                       out=brow[rb][0:1, colsS:2 * colsS],
                       in_=bc_sb[DS + s:DS + s + 1, 0:colsS]),
                   reads=(t_bc,), writes=(t_brow[rb],))
            kb.emit(pe, lambda e, rb=rb, bk=bk, colsS=colsS: e.matmul(
                P[bk].ap()[:, 0:2 * colsS], ones_sb[:], brow[rb][0:1, 0:2 * colsS],
                start=True, stop=True),
                reads=(t_brow[rb], t_consts), writes=(t_P[bk],))
            kb.emit(act, lambda e, sb_i=sb_i, bk=bk, colsS=colsS: e.copy(
                bcrep[sb_i][:, 0:2 * colsS], P[bk].ap()[:, 0:2 * colsS]),
                reads=(t_P[bk],), writes=(t_bcr[sb_i],))
        else:
            kb.emit(pe, lambda e, rb=rb, bk=bk, colsS=colsS: e.matmul(
                P[bk].ap()[:, 0:colsS], ones_sb[:], brow[rb][0:1, 0:colsS],
                start=True, stop=True),
                reads=(t_brow[rb], t_consts), writes=(t_P[bk],))
            kb.emit(act, lambda e, sb_i=sb_i, bk=bk, colsS=colsS: e.copy(
                bcrep[sb_i][:, 0:colsS], P[bk].ap()[:, 0:colsS]),
                reads=(t_P[bk],), writes=(t_bcr[sb_i],))
            kb.dma(sp, sem_row[rb],
                   lambda e, s=s, rb=rb, colsS=colsS: e.dma_start(
                       out=brow[rb][0:1, CS_MAX:CS_MAX + colsS],
                       in_=bc_sb[DS + s:DS + s + 1, 0:colsS]),
                   reads=(t_bc,), writes=(t_brow[rb],))
            kb.emit(pe, lambda e, rb=rb, bk=bk, colsS=colsS: e.matmul(
                P[bk].ap()[:, 0:colsS], ones_sb[:],
                brow[rb][0:1, CS_MAX:CS_MAX + colsS],
                start=True, stop=True),
                reads=(t_brow[rb], t_consts, t_bcr[sb_i]), writes=(t_P[bk],))
            kb.emit(act, lambda e, sb_i=sb_i, bk=bk, colsS=colsS: e.copy(
                bcrep[sb_i][:, CS_MAX:CS_MAX + colsS], P[bk].ap()[:, 0:colsS]),
                reads=(t_P[bk],), writes=(t_bcr[sb_i],))

    def em_extract(n, jb, s):
        b, c, s0, colsS, base2, last = params(n)

        def carry_fn(e, s=s, jb=jb, colsS=colsS):
            hv = h_rg.ap().rearrange("p (j cc) -> p j cc", cc=CS_MAX)
            src = hv[:, 0:4, colsS - 1:colsS]
            cv = carries.ap().rearrange("p (j s2) -> p j s2", s2=DS)
            dst = cv[:, jb * 4:(jb + 1) * 4, s:s + 1]
            return e.tensor_copy(dst, src)
        kb.emit(pool, carry_fn, reads=tuple(t_h), writes=(t_car[s][jb],))

    def em_scan_hs(n, hb, s, pend):
        b, c, s0, colsS, base2, last = params(n)
        sb_i = (hb * DS + s) % NBREP
        coff = colsS if 2 * colsS <= 512 else CS_MAX
        for gi in range(2):
            g4 = 2 * hb + gi
            js = list(range(g4 * 4, (g4 + 1) * 4))
            for j in js:
                ai = j % NRING
                kb.emit(act, lambda e, j=j, s=s, ai=ai, colsS=colsS: e.activation(
                    a_t[ai][:, 0:colsS], delta[j][:, 0:colsS], AF.Exp,
                    bias=0.0, scale=acol_sb[:, j * DS + s: j * DS + s + 1]),
                    reads=(t_dl[j], t_consts), writes=(t_a[ai],))
            if pend[0] is not None:
                em_extract(n, pend[0][0], pend[0][1])
            pend[0] = (g4, s)
            for j in js:
                ai = j % NRING
                kb.emit(dve, lambda e, j=j, ai=ai, sb_i=sb_i, colsS=colsS: e.tensor_mul(
                    b_t[ai][:, 0:colsS], wv[j][:, 0:colsS], bcrep[sb_i][:, 0:colsS]),
                    reads=(t_wv[j], t_bcr[sb_i]), writes=(t_b[ai],))
            for j in js:
                ai = j % NRING
                cidx = j * DS + s
                kb.emit(dve, lambda e, j=j, ai=ai, cidx=cidx, colsS=colsS:
                        e.tensor_tensor_scan(
                            hslot(j)[:, 0:colsS],
                            a_t[ai][:, 0:colsS], b_t[ai][:, 0:colsS],
                            carries.ap()[:, cidx:cidx + 1], ALU.mult, ALU.add),
                        reads=(t_a[ai], t_b[ai], t_car[s][g4]), writes=(t_h[ai],))
            for j in js:
                ai = j % NRING
                ceng = dve if j % 2 == 0 else pool
                yeng = ceng
                ci = j % 2
                kb.emit(ceng, lambda e, j=j, ci=ci, sb_i=sb_i, coff=coff, colsS=colsS:
                        e.tensor_mul(
                            cm_t[ci][:, 0:colsS],
                            hslot(j)[:, 0:colsS],
                            bcrep[sb_i][:, coff:coff + colsS]),
                        reads=(t_h[ai], t_bcr[sb_i]), writes=(t_cm[ci],))
                kb.emit(yeng, lambda e, j=j, ci=ci, colsS=colsS: e.tensor_add(
                    y_t[j][:, 0:colsS], y_t[j][:, 0:colsS], cm_t[ci][:, 0:colsS]),
                    reads=(t_cm[ci], t_y[j]), writes=(t_y[j],))

    def em_yfin(n):
        b, c, s0, colsS, base2, last = params(n)
        for j in range(NJ):
            kb.emit(dve, lambda e, j=j, base2=base2, colsS=colsS: e.tensor_mul(
                y_t[j][:, 0:colsS], y_t[j][:, 0:colsS],
                sz_buf[j][:, base2:base2 + colsS]),
                reads=(t_sz[j], t_y[j]), writes=(t_y[j],))
            split_cols(lambda j=j, colsS=colsS: y_t[j][:, 0:colsS],
                       ybh[j][:, 0:colsS], ybl[j][:, 0:colsS],
                       (t_y[j],), t_ybh[j], t_ybl[j])

    def pieces_wout(n):
        b, c, s0, colsS, base2, last = params(n)
        par = n % 2
        ps = []
        for blk in range(2):
            o0 = blk * 4
            st = {}

            def p_k(k, o0=o0, st=st, colsS=colsS):
                if 'ma' not in st:
                    st['ma'] = MultiAcc(kb, t_P)
                ma = st['ma']
                slot = k % 2
                kb.dma(sp, sem_wh[slot],
                       lambda e, k=k, slot=slot, o0=o0: e.dma_start(
                           out=wb_h[slot][:],
                           in_=wout_h[k * 128:(k + 1) * 128, o0 * 128:o0 * 128 + 512]),
                       reads=(t_w_wout,), writes=(t_wbh[slot],))
                kb.dma(sp, sem_wl[slot],
                       lambda e, k=k, slot=slot, o0=o0: e.dma_start(
                           out=wb_l[slot][:],
                           in_=wout_l[k * 128:(k + 1) * 128, o0 * 128:o0 * 128 + 512]),
                       reads=(t_w_wout,), writes=(t_wbl[slot],))
                for oi in range(4):
                    mm_hl3(ma, oi,
                           wb_h[slot][:, oi * 128:(oi + 1) * 128],
                           wb_l[slot][:, oi * 128:(oi + 1) * 128],
                           y_bf[k], CS_MAX, 0, colsS,
                           (t_wbh[slot],), (t_wbl[slot],), (t_ybh[k], t_ybl[k]),
                           first=(k == 0), last=(k == NJ - 1))
            for k in range(NJ):
                ps.append(lambda k=k, p_k=p_k: p_k(k))

            def p_evac(o0=o0, st=st, colsS=colsS, par=par):
                st['ma'].finish()
                for oi in range(4):
                    evac_add2(o0 + oi, par, oi, colsS)
            ps.append(p_evac)
        return ps

    def pieces_prodout(n):
        b, c, s0, colsS, base2, last = params(n)
        par = n % 2
        ps = []

        def p_prod():
            for o in range(NK):
                kb.emit(pool, lambda e, o=o, par=par, colsS=colsS: e.tensor_mul(
                    s_sb[par][o][:, 0:colsS], s_sb[par][o][:, 0:colsS],
                    g_sb[o][:, 0:colsS]),
                    reads=(t_g[o], t_s[par][o]), writes=(t_s[par][o],))
        ps.append(p_prod)

        nsub = (colsS + 127) // 128
        for ti in range(nsub):
            def p_sub(ti=ti, s0=s0, colsS=colsS, par=par, b=b):
                u0 = ti * 128
                Fz = min(128, colsS - u0)
                for rnd in range(2):
                    grp = []
                    for oi in range(4):
                        o = rnd * 4 + oi
                        def fn(e, o=o, oi=oi, u0=u0, Fz=Fz, par=par,
                               first=(oi == 0), last_=(oi == 3)):
                            return e.matmul(P[5].ap()[0:Fz, oi * 128:(oi + 1) * 128],
                                            s_sb[par][o][:, u0:u0 + Fz], ident_sb[:],
                                            start=first, stop=last_, is_transpose=True,
                                            skip_group_check=True)
                        grp.append((fn, [t_s[par][o], t_consts]))
                    kb.mm_group(grp, (t_P[5],))
                    kb.emit(act, lambda e, rnd=rnd, Fz=Fz: e.copy(
                        ostg.ap()[0:Fz, rnd * 512:(rnd + 1) * 512], P[5].ap()[0:Fz, :]),
                        reads=(t_P[5],), writes=(t_ostg,))
                tok0 = s0 + u0
                kb.dma(pool, sem_oacc,
                       lambda e, Fz=Fz, tok0=tok0, b=b: e.dma_start(
                           out=out[b, tok0:tok0 + Fz, :], in_=ostg[0:Fz, :],
                           accum_op=ALU.add),
                       reads=(t_ostg,), writes=(t_out_hbm,))
            ps.append(p_sub)
        return ps

    def pieces_ffn(n):
        b, c, s0, colsS, base2, last = params(n)
        ps = []
        for blk in range(2):
            o0 = blk * 4
            st = {}

            def p_k(k, o0=o0, st=st, colsS=colsS, base2=base2):
                if 'ma' not in st:
                    st['ma'] = MultiAcc(kb, t_P)
                ma = st['ma']
                slot = k % 2
                kb.dma(sp, sem_wh[slot],
                       lambda e, k=k, slot=slot, o0=o0: e.dma_start(
                           out=wb_h[slot][:],
                           in_=wtT_scr[k * 128:(k + 1) * 128, o0 * 128:o0 * 128 + 512]),
                       reads=(t_w_ffn,), writes=(t_wbh[slot],))
                for oi in range(4):
                    mm_hl1(ma, oi,
                           wb_h[slot][:, oi * 128:(oi + 1) * 128],
                           xn_hl[k], XNW, base2, colsS,
                           (t_wbh[slot],), (t_xnh[k], t_xnl[k]),
                           first=(k == 0), last=(k == NK - 1))
            for k in range(NK):
                ps.append(lambda k=k, p_k=p_k: p_k(k))

            def p_evac(o0=o0, st=st, colsS=colsS):
                st['ma'].finish()
                for oi in range(4):
                    o = o0 + oi
                    if 2 * colsS <= 512:
                        stg2, tstg2 = conv_acc[oi % 2], t_cacc[oi % 2]
                        evac_sum2(dve, lambda stg2=stg2, colsS=colsS: stg2.ap()[:, 0:colsS],
                                  oi, colsS, writes=(tstg2,))
                        kb.emit(act, lambda e, o=o, stg2=stg2, colsS=colsS: e.activation(
                            g_sb[o][:, 0:colsS], stg2.ap()[:, 0:colsS], AF.Silu,
                            bias=ffnb_sb[:, o:o + 1], scale=alpha_sb[:, 0:1]),
                            reads=(tstg2, t_consts), writes=(t_g[o],))
                    else:
                        kb.emit(act, lambda e, o=o, oi=oi, colsS=colsS: e.activation(
                            g_sb[o][:, 0:colsS], pacc(oi)[:, 0:colsS], AF.Silu,
                            bias=ffnb_sb[:, o:o + 1], scale=alpha_sb[:, 0:1]),
                            reads=(t_pacc[oi], t_consts), writes=(t_g[o],))
            ps.append(p_evac)
        return ps

    NQ = 2 * DS

    def scan_window(n, ps):
        rem = list(ps)
        pend = [None]
        LOOK = 3
        for si in range(NQ):
            hb, s = divmod(si, DS)
            if si == 0:
                for la in range(LOOK):
                    em_bcast(n, la // DS, la % DS)
            em_scan_hs(n, hb, s, pend)
            if si + LOOK < NQ:
                em_bcast(n, (si + LOOK) // DS, (si + LOOK) % DS)
            if rem:
                q = -(-len(rem) // max(1, NQ - si))
                for _ in range(min(q, len(rem))):
                    rem.pop(0)()
        em_extract(n, pend[0][0], pend[0][1])
        while rem:
            rem.pop(0)()

    for p in (pieces_input_ln(0) + pieces_transposes(0) + pieces_xz_x(0)
              + pieces_conv(0)):
        p()

    for n in range(NSTEP):
        if n > 0:
            em_yfin(n - 1)
        dblks = pieces_delta(n)
        for p in dblks[0] + dblks[1]:
            p()
        em_post_q(n, 0)
        em_post_q(n, 1)
        em_prefill(n)
        ps = []
        for qb in (2, 3):
            ps += dblks[qb]
            ps.append(lambda n=n, qb=qb: em_post_q(n, qb))
        ps += pieces_z(n)
        if n > 0:
            ps += pieces_wout(n - 1) + pieces_prodout(n - 1)
        ps += pieces_ffn(n)
        if n + 1 < NSTEP:
            ps += (pieces_input_ln(n + 1) + pieces_transposes(n + 1)
                   + pieces_xz_x(n + 1) + pieces_conv(n + 1))
        scan_window(n, ps)

    em_yfin(NSTEP - 1)
    for p in pieces_wout(NSTEP - 1) + pieces_prodout(NSTEP - 1):
        p()

    def final(sync):
        sync.wait_ge(sem_bulk, kb.dma_counts[id(sem_bulk)])
        if kb.dma_counts[id(sem_oacc)]:
            sync.wait_ge(sem_oacc, kb.dma_counts[id(sem_oacc)])
    kb.sp.thunks.append(final)

    with nc.Block() as block:
        @block.sync
        def _(sync):
            for th in kb.sp.thunks:
                th(sync)

        @block.tensor
        def _(te):
            for th in kb.pe.thunks:
                th(te)

        @block.vector
        def _(v):
            for th in kb.dve.thunks:
                th(v)

        @block.scalar
        def _(sc):
            for th in kb.act.thunks:
                th(sc)

        @block.gpsimd
        def _(g):
            for th in kb.pool.thunks:
                th(g)

    return nc



def make_in_map(inp_shard, W):
    a_cols = np.ascontiguousarray(
        np.asarray(W["L0_attn_A"]).reshape(NJ, 128, DS).transpose(1, 0, 2).reshape(128, -1))
    d_cols = np.ascontiguousarray(np.asarray(W["L0_attn_D"]).reshape(NJ, 128).T)
    convw = np.asarray(W["L0_attn_conv_w"])[:, 0, :]
    convw_cols = np.ascontiguousarray(convw.reshape(4, NJ, 128).transpose(2, 1, 0).reshape(128, -1))
    convb_cols = np.ascontiguousarray(np.asarray(W["L0_attn_conv_b"]).reshape(NJ, 128).T)
    bdt_cols = np.ascontiguousarray(np.asarray(W["L0_attn_b_dt"]).reshape(NJ, 128).T)
    ffnb_cols = np.ascontiguousarray(np.asarray(W["L0_ffn_bias"]).reshape(NK, 128).T)
    alpha_bc = np.ascontiguousarray(
        np.broadcast_to(np.asarray(W["L0_ffn_alpha"]).reshape(1, 1), (128, 1)))
    return {
        "inp": np.ascontiguousarray(inp_shard, dtype=np.float32),
        "w_in": np.ascontiguousarray(np.asarray(W["L0_attn_W_in"], dtype=np.float32)),
        "w_x": np.ascontiguousarray(np.asarray(W["L0_attn_W_x"], dtype=np.float32)),
        "w_dt": np.ascontiguousarray(np.asarray(W["L0_attn_W_dt"], dtype=np.float32)),
        "w_out": np.ascontiguousarray(np.asarray(W["L0_attn_W_out"], dtype=np.float32)),
        "ffn_w": np.ascontiguousarray(np.asarray(W["L0_ffn_weight"], dtype=np.float32)),
        "a_cols": a_cols.astype(np.float32),
        "d_cols": d_cols.astype(np.float32),
        "convw_cols": convw_cols.astype(np.float32),
        "convb_cols": convb_cols.astype(np.float32),
        "bdt_cols": bdt_cols.astype(np.float32),
        "ffnb_cols": ffnb_cols.astype(np.float32),
        "alpha_bc": alpha_bc.astype(np.float32),
        "ident_in": np.eye(128, dtype=np.float32),
        "ones_in": np.ones((1, 128), dtype=np.float32),
        "onescol_in": np.ones((128, 1), dtype=np.float32),
    }


def kernel(**inputs):
    full = inputs["L0_attn_inp"]
    BATCH, T = full.shape[0], full.shape[1]
    B = BATCH // NCORES
    nc = build_nc(B, T, 256)
    in_maps = [make_in_map(full[ci * B:(ci + 1) * B], inputs) for ci in range(NCORES)]
    res = run_bass_kernel_spmd(nc, in_maps, core_ids=list(range(NCORES)))
    return np.concatenate([np.asarray(r["out"]) for r in res.results], axis=0)
